# version 08
# speedup vs baseline: 1.4673x; 1.2154x over previous
import contextlib
import ml_dtypes
import numpy as np

import concourse.bass as bass
import concourse.mybir as mybir
from concourse.bass_utils import run_bass_kernel_spmd

F32 = mybir.dt.float32
BF16 = mybir.dt.bfloat16
AF = mybir.ActivationFunctionType
ALU = mybir.AluOpType
AX = mybir.AxisListType.X

NCORE = 8
T = 2048
D = 1024
NDC = 8
TC = 512
NTC = T // TC
PAIRS = [[0, 1], [2, 3], [4, 5], [6, 7]]

LN_EPS_OUTER = 1e-5

DEV_STOP = None
DEV_DUMP = None
DEV_NTC = None

CONST_VALS = [1e-5, 1e-6, -60.0, -4.0, 2.0, 1.0, -1.0, 0.5]



def pack_x_half(x, core):
    b, h = core // 2, core % 2
    xh = np.ascontiguousarray(np.asarray(x)[b, h * T:(h + 1) * T, :])
    xt = np.ascontiguousarray(xh.T)
    return np.ascontiguousarray(
        xt.reshape(NDC, 128, T).transpose(1, 0, 2).reshape(128, NDC * T))


def unpack_x_half(p):
    xt = p.reshape(128, NDC, T).transpose(1, 0, 2).reshape(D, T)
    return np.ascontiguousarray(xt.T)


def pack_km(wkm, mtile=128):
    wkm = np.asarray(wkm)
    K, M = wkm.shape
    assert K % 128 == 0 and M % mtile == 0
    nkc, noc = K // 128, M // mtile
    arr = wkm.reshape(nkc, 128, noc, mtile).transpose(1, 2, 0, 3)
    return np.ascontiguousarray(arr.reshape(128, noc * nkc * mtile).astype(np.float32))


def pack_bias_part(b):
    b = np.asarray(b)
    M = b.shape[0]
    assert M % 128 == 0
    return np.ascontiguousarray(b.reshape(M // 128, 128).T.astype(np.float32))


def tp(a):
    return np.ascontiguousarray(np.asarray(a).T.astype(np.float32))



class Sem:
    def __init__(self, stack, nc, name):
        self.h = stack.enter_context(nc.semaphore(name))
        self.c = 0

    def inc(self, instr, n=1):
        instr.then_inc(self.h, n)
        self.c += n
        return instr

    def tok(self):
        return (self, self.c)


class Rec:
    __slots__ = ("incs",)

    def __init__(self):
        self.incs = []

    def then_inc(self, sem_h, n=1):
        self.incs.append((sem_h, n))
        return self


_NO_AUTO = {"wait_ge", "dma_start", "dma_start_transpose", "collective_compute",
            "indirect_dma_start", "nop"}

_OUT_KEYS = ("out", "out_ap", "accum_out")


def _ap_range(a):
    try:
        t = a.tensor
        off = a.offset
        span = 1
        for step, count in list(a.ap)[1:]:
            span += (count - 1) * step
        return (t.name, off, off + span)
    except Exception:
        return None


class Eng:

    def __init__(self, block, ename):
        self.block = block
        self.ename = ename
        self.pend = []
        self._waited = {}
        self.autosem = None
        self.dep = {}

    def wait(self, token):
        if token is None:
            return
        sem, v = token
        if v <= 0 or self._waited.get(id(sem), 0) >= v:
            return
        self._waited[id(sem)] = v
        self.pend.append(("wait_ge", (sem.h, v), {}, None))

    def flush(self):
        if not self.pend:
            return
        ops = self.pend
        self.pend = []

        def run(q, ops=ops):
            for name, a, k, r in ops:
                ins = getattr(q, name)(*a, **k)
                if r is not None:
                    for sem_h, n in r.incs:
                        ins.then_inc(sem_h, n)

        getattr(self.block, self.ename)(run)

    def __getattr__(self, name):
        if name.startswith("_"):
            raise AttributeError(name)

        def f(*a, **k):
            auto = self.autosem is not None and name not in _NO_AUTO
            outs, ins = [], []
            if auto:
                for kk, v in k.items():
                    rng = _ap_range(v)
                    if rng is None:
                        continue
                    (outs if kk in _OUT_KEYS else ins).append(rng)
                for v in a:
                    rng = _ap_range(v)
                    if rng is not None:
                        outs.append(rng)
                        ins.append(rng)
                need = 0
                dep = self.dep
                for nm, s0, e0 in ins:
                    ent = dep.get(nm)
                    if ent is None:
                        continue
                    need = max(need, ent[2])
                    for ss, ee, idx in ent[0]:
                        if ss < e0 and s0 < ee and idx > need:
                            need = idx
                for nm, s0, e0 in outs:
                    ent = dep.get(nm)
                    if ent is None:
                        continue
                    need = max(need, ent[2], ent[3])
                    for lst in (ent[0], ent[1]):
                        for ss, ee, idx in lst:
                            if ss < e0 and s0 < ee and idx > need:
                                need = idx
                if need > 0:
                    self.wait((self.autosem, need))
            r = Rec()
            self.pend.append((name, a, k, r))
            if auto:
                self.autosem.c += 1
                my = self.autosem.c
                dep = self.dep
                for nm, s0, e0 in outs:
                    ent = dep.setdefault(nm, [[], [], 0, 0])
                    ent[0].append((s0, e0, my))
                    if len(ent[0]) > 48:
                        ent[2] = max(ent[2], max(i for _, _, i in ent[0][:-32]))
                        ent[0] = ent[0][-32:]
                for nm, s0, e0 in ins:
                    ent = dep.setdefault(nm, [[], [], 0, 0])
                    ent[1].append((s0, e0, my))
                    if len(ent[1]) > 48:
                        ent[3] = max(ent[3], max(i for _, _, i in ent[1][:-32]))
                        ent[1] = ent[1][-32:]
                r.incs.append((self.autosem.h, 1))
            return r

        return f


class Builder:
    def __init__(self):
        self.nc = bass.Bass("TRN2", target_bir_lowering=False)
        self.stack = contextlib.ExitStack()
        self.feeds = {}
        self._dram = {}
        self._n = 0
        self.block = self.stack.enter_context(self.nc.Block())
        self.PE = Eng(self.block, "tensor")
        self.DVE = Eng(self.block, "vector")
        self.ACT = Eng(self.block, "scalar")
        self.POOL = Eng(self.block, "gpsimd")
        self.SP = Eng(self.block, "sync")
        self.engines = [self.PE, self.DVE, self.ACT, self.POOL, self.SP]
        self.sPE = Sem(self.stack, self.nc, "sPE")
        self.sDVE = Sem(self.stack, self.nc, "sDVE")
        self.sACT = Sem(self.stack, self.nc, "sACT")
        self.sPOOL = Sem(self.stack, self.nc, "sPOOL")
        self.sDW = Sem(self.stack, self.nc, "sDW")
        self.sDW2 = Sem(self.stack, self.nc, "sDW2")
        self.sDX = Sem(self.stack, self.nc, "sDX")
        self.sOUT = Sem(self.stack, self.nc, "sOUT")
        self.sSPL = Sem(self.stack, self.nc, "sSPL")
        self.DVE.autosem = self.sDVE
        self.ACT.autosem = self.sACT
        self.POOL.autosem = self.sPOOL
        self._ncc = 0
        self.all_sems = [self.sPE, self.sDVE, self.sACT, self.sPOOL,
                         self.sDW, self.sDW2, self.sDX, self.sOUT, self.sSPL]

    def din(self, name, arrs, dtype=F32):
        if isinstance(arrs, np.ndarray):
            arrs = [arrs] * NCORE
        assert name not in self._dram, name
        h = self.nc.dram_tensor(name, list(arrs[0].shape), dtype, kind="ExternalInput")
        self.feeds[name] = arrs
        self._dram[name] = h.ap()
        return self._dram[name]

    def dout(self, name, shp, dtype=F32):
        h = self.nc.dram_tensor(name, list(shp), dtype, kind="ExternalOutput")
        self._dram[name] = h.ap()
        return self._dram[name]

    def dint(self, name, shp, dtype=F32):
        h = self.nc.dram_tensor(name, list(shp), dtype)
        self._dram[name] = h.ap()
        return self._dram[name]

    def cc_sem(self):
        s = Sem(self.stack, self.nc, f"scc{self._ncc}")
        self._ncc += 1
        self.all_sems.append(s)
        return s

    def sb(self, st, name, shp, dtype=F32, side="left"):
        self._n += 1
        return st.enter_context(
            self.nc.sbuf_tensor(f"{name}_{self._n}", list(shp), dtype, side=side))

    def ps(self, st, name, shp, dtype=F32):
        self._n += 1
        return st.enter_context(
            self.nc.psum_tensor(f"{name}_{self._n}", list(shp), dtype))

    def split_hl(self, row_f32, hi_bf, lo_bf):
        (self.DVE.tensor_copy(out=hi_bf, in_=row_f32))
        (self.DVE.tensor_tensor(out=lo_bf, in0=row_f32, in1=hi_bf,
                                op=ALU.subtract))
        return self.sDVE.tok()

    def bcast_hl(self, ps_ap, hi_bf, lo_bf, parts=128):
        self.PE.matmul(out=ps_ap, lhsT=self.c_ones1b[0:1, 0:parts], rhs=hi_bf,
                       start=True, stop=False)
        ins = self.PE.matmul(out=ps_ap, lhsT=self.c_ones1b[0:1, 0:parts],
                             rhs=lo_bf, start=False, stop=True)
        self.sPE.inc(ins)
        return self.sPE.tok()

    def dma(self, eng, sem, out, in_):
        eng.wait(sem.tok())
        sem.inc(eng.dma_start(out=out, in_=in_), 16)
        return sem.tok()

    def flush(self):
        for e in self.engines:
            e.flush()

    def scope(self):
        B = self

        class _S(contextlib.ExitStack):
            def __exit__(self, *exc):
                B.flush()
                return contextlib.ExitStack.__exit__(self, *exc)

            def close(self):
                B.flush()
                contextlib.ExitStack.close(self)

        return _S()

    def barrier(self):
        for e in self.engines:
            for s in self.all_sems:
                e.wait(s.tok())

    def mm_acc(self, psum_ap, tiles, sem=None):
        last = len(tiles) - 1
        ins = None
        for i, (lh, rh) in enumerate(tiles):
            ins = self.PE.matmul(out=psum_ap, lhsT=lh, rhs=rh,
                                 start=(i == 0), stop=(i == last))
        if sem is not None:
            sem.inc(ins)
        return ins

    def fm_proj(self, st, name, wslice, nkc, noc, rhs, evac, ntc=NTC, mtile=128,
                entry_barrier=True, ncol=TC):
        if entry_barrier:
            self.barrier()
        wb = [self.sb(st, f"{name}_w{i}", [128, nkc * mtile], BF16) for i in range(2)]
        psb = [self.ps(st, f"{name}_p{i}", [128, TC]) for i in range(2)]
        wtok = [None, None]
        free = [None, None]
        wsem = [self.sDW, self.sDW2]
        pe_after_oc = {}
        wtok[0] = self.dma(self.POOL, wsem[0], wb[0][:], wslice(0))
        for oc in range(noc):
            slot = oc % 2
            if oc + 1 < noc:
                ns = (oc + 1) % 2
                if oc >= 1:
                    self.POOL.wait(pe_after_oc[oc - 1])
                wtok[ns] = self.dma(self.POOL, wsem[ns], wb[ns][:], wslice(oc + 1))
            for tc in range(ntc):
                bank = (oc * ntc + tc) % 2
                self.PE.wait(wtok[slot])
                self.PE.wait(free[bank])
                tiles = [(wb[slot][:, kc * mtile:(kc + 1) * mtile], rhs(kc, tc))
                         for kc in range(nkc)]
                self.mm_acc(psb[bank][0:128, 0:ncol], tiles, sem=self.sPE)
                free[bank] = evac(oc, tc, psb[bank], self.sPE.tok())
            pe_after_oc[oc] = self.sPE.tok()


def x_chunk(B, dc, tc):
    col = dc * T + tc * TC
    return B.X[:, col:col + TC]


def xn_chunk(B, dc, tc):
    col = dc * T + tc * TC
    return B.XN[:, col:col + TC]


def emit_ln_stats(B, st, src_get, src_is_bf16, eps, stats_cb=None,
                  xn_out=None, x32=None, name="ln", tcs=None, xn_stride=T,
                  gb=None):
    PE, DVE, ACT = B.PE, B.DVE, B.ACT
    ps_sum = B.ps(st, name + "_pssum", [1, TC])
    ps_sq = B.ps(st, name + "_pssq", [1, TC])
    ps_m = B.ps(st, name + "_psm", [128, TC])
    ps_r = B.ps(st, name + "_psr", [128, TC])
    rows = B.sb(st, name + "_rows", [1, 4 * TC], side="right")
    rhl = B.sb(st, name + "_rhl", [1, 4 * TC], BF16, side="right")
    xb = [B.sb(st, name + f"_xb{i}", [128, TC], BF16) for i in range(2)]
    sq = [B.sb(st, name + f"_sq{i}", [128, TC], BF16) for i in range(2)]
    tmp = B.sb(st, name + "_tmp", [128, TC])
    if tcs is None:
        tcs = range(DEV_NTC if DEV_NTC else NTC)
    for tc in tcs:
        for dc in range(NDC):
            src = src_get(dc, tc)
            par = dc % 2
            if dc >= 2 or tc > 0:
                ACT.wait((B.sPE, B.sPE.c - 1))
            if src_is_bf16:
                xb_ap = src
            else:
                (ACT.copy(out=xb[par][:], in_=src))
                xb_ap = xb[par][:]
            (ACT.activation(out=sq[par][:], in_=src, func=AF.Square))
            PE.wait(B.sACT.tok())
            PE.matmul(out=ps_sum[:], lhsT=B.c_ones16[:], rhs=xb_ap,
                      start=(dc == 0), stop=(dc == NDC - 1))
            ins = PE.matmul(out=ps_sq[:], lhsT=B.c_ones16[:], rhs=sq[par][:],
                            start=(dc == 0), stop=(dc == NDC - 1))
            B.sPE.inc(ins)
        m = rows[0:1, 0:TC]
        msq = rows[0:1, TC:2 * TC]
        var = rows[0:1, 2 * TC:3 * TC]
        r = rows[0:1, 3 * TC:4 * TC]
        ACT.wait(B.sPE.tok())
        ACT.wait(B.sDVE.tok())
        (ACT.activation(out=m, in_=ps_sum[:], func=AF.Copy, scale=1.0 / D))
        (ACT.activation(out=msq, in_=ps_sq[:], func=AF.Copy, scale=1.0 / D))
        DVE.wait(B.sACT.tok())
        (DVE.tensor_tensor(out=var, in0=m, in1=m, op=ALU.mult))
        (DVE.tensor_tensor(out=var, in0=msq, in1=var, op=ALU.subtract))
        ACT.wait(B.sDVE.tok())
        (ACT.activation(out=var, in_=var, func=AF.Sqrt, bias=eps))
        DVE.wait(B.sACT.tok())
        (DVE.reciprocal(out=r, in_=var))
        if stats_cb is not None:
            stats_cb(tc, m, r)
        if DEV_STOP == "stats":
            continue
        if xn_out is not None:
            mh, ml = rhl[0:1, 0:TC], rhl[0:1, TC:2 * TC]
            rh, rl = rhl[0:1, 2 * TC:3 * TC], rhl[0:1, 3 * TC:4 * TC]
            DVE.wait(B.sPE.tok())
            B.split_hl(m, mh, ml)
            B.split_hl(r, rh, rl)
            PE.wait(B.sDVE.tok())
            B.bcast_hl(ps_m[:], mh, ml)
            B.bcast_hl(ps_r[:], rh, rl)
            DVE.wait(B.sPE.tok())
            for dc in range(NDC):
                (DVE.tensor_tensor(out=tmp[:], in0=x32(dc, tc),
                                             in1=ps_m[:], op=ALU.subtract))
                col = dc * xn_stride + tc * TC
                if gb is None:
                    (DVE.tensor_tensor(out=xn_out[:, col:col + TC], in0=tmp[:],
                                                 in1=ps_r[:], op=ALU.mult))
                else:
                    (DVE.tensor_tensor(out=tmp[:], in0=tmp[:], in1=ps_r[:],
                                       op=ALU.mult))
                    g_ap, b_ap = gb(dc)
                    (DVE.tensor_scalar(out=xn_out[:, col:col + TC], in0=tmp[:],
                                       scalar1=g_ap, scalar2=b_ap,
                                       op0=ALU.mult, op1=ALU.add))
            PE.wait(B.sDVE.tok())
    return B.sDVE.tok()



def host_prep_l0(B, inp):
    B.din("l0_wqT", pack_km(tp(inp["L0_attn_Wq"])))
    B.din("l0_wkT", pack_km(tp(inp["L0_attn_Wk"])))
    B.din("l0_wvT", pack_km(tp(inp["L0_attn_Wv"])))
    B.din("l0_wcT", pack_km(tp(inp["L0_attn_Wc"])))
    B.din("l0_battn", np.concatenate(
        [pack_bias_part(inp["L0_attn_b" + k]) for k in "qkvc"], axis=1))
    B.din("l0_rwT", pack_km(tp(inp["L0_ffn_router_w"]), mtile=8))
    B.din("l0_rb", np.asarray(inp["L0_ffn_router_b"]).reshape(1, 8).astype(np.float32))
    B.din("l0_lncons", np.concatenate(
        [pack_bias_part(inp["L0_ffn_router_ln_scale"]),
         pack_bias_part(inp["L0_ffn_router_ln_bias"])]
        + [pack_bias_part(np.asarray(inp["L0_ffn_expert_ln_scale"])[e]) for e in range(8)]
        + [pack_bias_part(np.asarray(inp["L0_ffn_expert_ln_bias"])[e]) for e in range(8)]
        + [pack_bias_part(np.asarray(inp["L0_ffn_b1"])[e]) for e in range(8)]
        + [pack_bias_part(np.asarray(inp["L0_ffn_b2"])[e]) for e in range(8)], axis=1))
    w1 = np.asarray(inp["L0_ffn_w1"])
    w2 = np.asarray(inp["L0_ffn_w2"])
    B.din("l0_w1", np.stack([pack_km(tp(w1[e])) for e in range(8)]))
    B.din("l0_w2", np.stack([pack_km(tp(w2[e])) for e in range(8)]))


def emit_l0_attn(B, lst):
    PE, DVE, ACT, POOL = B.PE, B.DVE, B.ACT, B.POOL
    H = 16
    battn = B.sb(lst, "l0_battn_sb", [128, 32], side="right")
    B.dma(B.SP, B.sDX, battn[:], B._dram["l0_battn"][:, :])
    kv_in = B.dint("l0_kv_in", [64, H * 65])
    kv_out = B.dint("l0_kv_out", [64, H * 65])
    scc = B.cc_sem()

    with B.scope() as st:
        QY = B.sb(st, "l0_qy", [128, NDC * T], BF16)
        KVS = B.sb(st, "l0_kvs", [64, H * 65])
        KVS16 = B.sb(st, "l0_kvs16", [128, H * 65], BF16)
        tfp = B.sb(st, "l0_tfp", [128, TC])
        tf2 = B.sb(st, "l0_tf2", [128, TC])

        def elu1(psum_slice, bias_ap, out_ap, pe_tok):
            ACT.wait(pe_tok)
            ACT.wait(B.sDVE.tok())
            r0 = psum_slice.shape[0] if hasattr(psum_slice, "shape") else 128
            (ACT.activation(out=tfp[0:r0, :], in_=psum_slice,
                                      func=AF.Relu, bias=bias_ap))
            DVE.wait(pe_tok)
            DVE.wait(B.sACT.tok())
            (DVE.tensor_scalar(out=tf2[0:r0, :], in0=psum_slice,
                                         scalar1=bias_ap, scalar2=0.0,
                                         op0=ALU.add, op1=ALU.min))
            ACT.wait(B.sDVE.tok())
            (ACT.activation(out=tf2[0:r0, :], in_=tf2[0:r0, :], func=AF.Exp))
            DVE.wait(B.sACT.tok())
            (DVE.tensor_tensor(out=out_ap, in0=tfp[0:r0, :],
                                         in1=tf2[0:r0, :], op=ALU.add))
            return B.sDVE.tok()

        wq = B._dram["l0_wqT"]
        if DEV_STOP == "q0":
            return
        with B.scope() as stq:
            B.fm_proj(stq, "l0q", lambda oc: wq[:, oc * 1024:(oc + 1) * 1024],
                      NDC, NDC, lambda kc, tc: xn_chunk(B, kc, tc),
                      lambda oc, tc, bank, tok: elu1(
                          bank[:], battn[:, oc:oc + 1],
                          QY[:, oc * T + tc * TC:oc * T + tc * TC + TC], tok))
        if DEV_STOP == "q":
            B.barrier()
            for dc in range(NDC):
                (ACT.copy(out=B.S[:, dc * T:(dc + 1) * T], in_=QY[:, dc * T:(dc + 1) * T]))
            B.barrier()
            return
        wk, wv = B._dram["l0_wkT"], B._dram["l0_wvT"]
        with B.scope() as st2:
            KO = B.sb(st2, "l0_ko", [128, T], BF16)
            VO = B.sb(st2, "l0_vo", [128, T], BF16)
            wbk = [B.sb(st2, f"l0_wbk{i}", [128, 1024], BF16) for i in range(2)]
            wbv = [B.sb(st2, f"l0_wbv{i}", [128, 1024], BF16) for i in range(2)]
            KTM = [B.sb(st2, f"l0_ktm{i}", [128, 64], BF16) for i in range(2)]
            VTM = [B.sb(st2, f"l0_vtm{i}", [128, 65], BF16) for i in range(2)]
            psb = [B.ps(st2, f"l0_psb{i}", [128, TC]) for i in range(2)]
            pst = [B.ps(st2, f"l0_pst{i}", [128, 64], BF16) for i in range(2)]
            pskv = [B.ps(st2, f"l0_pskv{i}", [64, 65]) for i in range(2)]
            B.barrier()
            (DVE.memset(VTM[0][:, 64:65], 1.0))
            (DVE.memset(VTM[1][:, 64:65], 1.0))
            wtok = {}
            wsem = [B.sDW, B.sDW2]
            for pre in range(2):
                B.dma(POOL, wsem[pre], wbk[pre][:], wk[:, pre * 1024:(pre + 1) * 1024])
                wtok[pre] = B.dma(POOL, wsem[pre], wbv[pre][:],
                                  wv[:, pre * 1024:(pre + 1) * 1024])
            for oc in range(NDC):
                slot = oc % 2
                for tc in range(NTC):
                    bank = tc % 2
                    PE.wait(wtok[oc])
                    PE.wait(B.sDVE.tok())
                    PE.wait(B.sACT.tok())
                    tiles = [(wbk[slot][:, kc * 128:(kc + 1) * 128], xn_chunk(B, kc, tc))
                             for kc in range(NDC)]
                    B.mm_acc(psb[bank][:], tiles, sem=B.sPE)
                    elu1(psb[bank][:], battn[:, 8 + oc:8 + oc + 1],
                         KO[:, tc * TC:(tc + 1) * TC], B.sPE.tok())
                    PE.wait(B.sDVE.tok())
                    PE.wait(B.sACT.tok())
                    tiles = [(wbv[slot][:, kc * 128:(kc + 1) * 128], xn_chunk(B, kc, tc))
                             for kc in range(NDC)]
                    B.mm_acc(psb[bank][:], tiles, sem=B.sPE)
                    ACT.wait(B.sPE.tok())
                    (ACT.activation(out=VO[:, tc * TC:(tc + 1) * TC],
                                              in_=psb[bank][:], func=AF.Identity,
                                              bias=battn[:, 16 + oc:16 + oc + 1]))
                for sub in range(2):
                    h = oc * 2 + sub
                    p0 = sub * 64
                    for tt in range(T // 128):
                        par = tt % 2
                        PE.wait(B.sACT.tok())
                        PE.wait(B.sDVE.tok())
                        B.sPE.inc(PE.transpose(
                            out=pst[par][:, 0:64],
                            in_=KO[p0:p0 + 64, tt * 128:(tt + 1) * 128],
                            identity=B.c_id64s[p0:p0 + 64, 0:64]))
                        ACT.wait(B.sPE.tok())
                        (ACT.copy(out=KTM[par][:, :], in_=pst[par][:, 0:64]))
                        PE.wait(B.sACT.tok())
                        B.sPE.inc(PE.transpose(
                            out=pst[par][:, 0:64],
                            in_=VO[p0:p0 + 64, tt * 128:(tt + 1) * 128],
                            identity=B.c_id64s[p0:p0 + 64, 0:64]))
                        ACT.wait(B.sPE.tok())
                        (ACT.copy(out=VTM[par][:, 0:64], in_=pst[par][:, 0:64]))
                        PE.wait(B.sACT.tok())
                        B.sPE.inc(PE.matmul(out=pskv[sub][:], lhsT=KTM[par][:, :],
                                            rhs=VTM[par][:, :], start=(tt == 0),
                                            stop=(tt == T // 128 - 1)))
                    ACT.wait(B.sPE.tok())
                    (ACT.copy(out=KVS[:, h * 65:(h + 1) * 65],
                                        in_=pskv[sub][:]))
                if oc + 2 < NDC:
                    POOL.wait(B.sPE.tok())
                    B.dma(POOL, wsem[slot], wbk[slot][:],
                          wk[:, (oc + 2) * 1024:(oc + 3) * 1024])
                    wtok[oc + 2] = B.dma(POOL, wsem[slot], wbv[slot][:],
                                         wv[:, (oc + 2) * 1024:(oc + 3) * 1024])

        if DEV_STOP == "kv":
            B.barrier()
            return
        POOL.wait(B.sACT.tok())
        B.dma(POOL, B.sDW, kv_in[:, :], KVS[:, :])
        POOL.wait(B.sDW.tok())
        scc.inc(POOL.collective_compute(
            "AllReduce", ALU.add, replica_groups=PAIRS,
            ins=[kv_in[:, :]], outs=[kv_out[:, :]]))
        POOL.wait(scc.tok())
        B.dma(POOL, B.sDW, KVS[:, :], kv_out[:, :])
        DVE.wait(B.sDW.tok())
        (DVE.tensor_copy(out=KVS16[0:64, :], in_=KVS[:, :]))
        (DVE.tensor_copy(out=KVS16[64:128, :], in_=KVS[:, :]))

        if DEV_STOP == "xchg":
            B.barrier()
            return
        with B.scope() as st3:
            ZB = B.sb(st3, "l0_zb", [64, TC])
            rowz = B.sb(st3, "l0_rowz", [1, 2 * TC], side="right")
            rzhl = B.sb(st3, "l0_rzhl", [1, 2 * TC], BF16, side="right")
            psy = [B.ps(st3, f"l0_psy{i}", [128, TC]) for i in range(2)]
            psz = B.ps(st3, "l0_psz", [64, TC])
            B.barrier()
            for h in range(H):
                oc, sub = h // 2, h % 2
                p0 = sub * 64
                for tc in range(NTC):
                    bank = (h * NTC + tc) % 2
                    PE.wait(B.sDVE.tok())
                    col = oc * T + tc * TC
                    B.sPE.inc(PE.matmul(out=psy[bank][0:65, :],
                                        lhsT=KVS16[p0:p0 + 64, h * 65:(h + 1) * 65],
                                        rhs=QY[p0:p0 + 64, col:col + TC],
                                        start=True, stop=True))
                    DVE.wait(B.sPE.tok())
                    (DVE.tensor_scalar(out=rowz[0:1, 0:TC],
                                                 in0=psy[bank][64:65, :],
                                                 scalar1=1e-6, scalar2=None,
                                                 op0=ALU.add))
                    (DVE.reciprocal(out=rowz[0:1, TC:2 * TC],
                                              in_=rowz[0:1, 0:TC]))
                    B.split_hl(rowz[0:1, TC:2 * TC], rzhl[0:1, 0:TC],
                               rzhl[0:1, TC:2 * TC])
                    PE.wait(B.sDVE.tok())
                    B.bcast_hl(psz[:], rzhl[0:1, 0:TC], rzhl[0:1, TC:2 * TC],
                               parts=64)
                    ACT.wait(B.sPE.tok())
                    (ACT.copy(out=ZB[:, :], in_=psz[:]))
                    DVE.wait(B.sACT.tok())
                    (DVE.tensor_tensor(out=QY[p0:p0 + 64, col:col + TC],
                                                 in0=psy[bank][0:64, :], in1=ZB[:, :],
                                                 op=ALU.mult))
        if DEV_STOP == "y":
            B.barrier()
            return
        wc = B._dram["l0_wcT"]

        def ev_s(oc, tc, bank, pe_tok):
            ACT.wait(pe_tok)
            col = oc * T + tc * TC
            (ACT.activation(out=B.S[:, col:col + TC], in_=bank[:],
                                      func=AF.Identity, bias=battn[:, 24 + oc:24 + oc + 1]))
            return B.sACT.tok()

        with B.scope() as st4:
            B.fm_proj(st4, "l0c", lambda oc: wc[:, oc * 1024:(oc + 1) * 1024],
                      NDC, NDC,
                      lambda kc, tc: QY[:, kc * T + tc * TC:kc * T + tc * TC + TC],
                      ev_s)
    B.barrier()


def emit_l0_ffn(B, lst, st_xn):
    PE, DVE, ACT, POOL = B.PE, B.DVE, B.ACT, B.POOL
    NHC = 32
    TG = 512
    NTG = T // TG

    cons = B.sb(lst, "l0_ffncons", [128, 464], side="right")
    B.dma(B.SP, B.sDX, cons[:], B._dram["l0_lncons"][:, :])
    AB = B.sb(lst, "l0_ab", [128, 2 * T], side="right")
    LGALL = B.sb(lst, "l0_lgall", [128, 8 * (T // 128)], BF16, side="right")
    LGALL2 = B.sb(lst, "l0_lgall2", [128, 8 * (T // 128)], BF16, side="right")

    with B.scope() as st:
        psb = B.ps(st, "l0_abps", [128, TC])
        rowsc = B.sb(st, "l0_rowsc", [1, 2 * TC], side="right")
        rhl2 = B.sb(st, "l0_rhl2", [1, 4 * TC], BF16, side="right")

        def cb(tc, m2, r2):
            a = rowsc[0:1, 0:TC]
            bb = rowsc[0:1, TC:2 * TC]
            DVE.wait(B.sACT.tok())
            (DVE.tensor_tensor(
                out=a, in0=B.r1row[0:1, tc * TC:(tc + 1) * TC], in1=r2, op=ALU.mult))
            (DVE.tensor_tensor(
                out=bb, in0=a, in1=B.m1row[0:1, tc * TC:(tc + 1) * TC], op=ALU.mult))
            (DVE.tensor_tensor(out=r2, in0=r2, in1=m2, op=ALU.mult))
            (DVE.tensor_tensor(out=bb, in0=bb, in1=r2, op=ALU.add))
            (DVE.tensor_scalar(out=bb, in0=bb, scalar1=-1.0, scalar2=None,
                                         op0=ALU.mult))
            B.split_hl(a, rhl2[0:1, 0:TC], rhl2[0:1, TC:2 * TC])
            B.split_hl(bb, rhl2[0:1, 2 * TC:3 * TC], rhl2[0:1, 3 * TC:4 * TC])
            PE.wait(B.sDVE.tok())
            B.bcast_hl(psb[:], rhl2[0:1, 0:TC], rhl2[0:1, TC:2 * TC])
            ACT.wait(B.sPE.tok())
            (ACT.copy(out=AB[:, tc * TC:(tc + 1) * TC], in_=psb[:]))
            PE.wait(B.sACT.tok())
            B.bcast_hl(psb[:], rhl2[0:1, 2 * TC:3 * TC], rhl2[0:1, 3 * TC:4 * TC])
            ACT.wait(B.sPE.tok())
            (ACT.copy(out=AB[:, T + tc * TC:T + (tc + 1) * TC], in_=psb[:]))
            PE.wait(B.sACT.tok())

        emit_ln_stats(B, st, lambda dc, tc: xn_chunk(B, dc, tc), True, 1e-6,
                      stats_cb=cb, name="l0s2")
    B.barrier()
    st_xn.close()

    def ab_a(tc4):
        return AB[:, tc4 * TC:(tc4 + 1) * TC]

    def ab_b(tc4):
        return AB[:, T + tc4 * TC:T + (tc4 + 1) * TC]

    with B.scope() as st:
        rin = B.sb(st, "l0_rin", [128, 128], BF16)
        xh = B.sb(st, "l0_xh", [128, 128])
        s8 = B.sb(st, "l0_s8", [128, 16])
        lg = B.sb(st, "l0_lg", [128, 8])
        sc = B.sb(st, "l0_sc", [128, 6])
        rw = B.sb(st, "l0_rw_sb", [128, 64], BF16)
        rbrow = B.sb(st, "l0_rbrow", [1, 8], side="right")
        rbhl = B.sb(st, "l0_rbhl", [1, 16], BF16, side="right")
        psr = B.ps(st, "l0_psr", [128, 8])
        B.barrier()
        B.dma(POOL, B.sDW, rw[:], B._dram["l0_rwT"][:, :])
        B.dma(B.SP, B.sDX, rbrow[:], B._dram["l0_rb"][:, :])
        B.barrier()
        B.split_hl(rbrow[0:1, :], rbhl[0:1, 0:8], rbhl[0:1, 8:16])
        B.barrier()
        for tt in range(T // 128):
            tc4 = (tt * 128) // TC
            off = (tt * 128) % TC
            for dc in range(NDC):
                DVE.wait(B.sPE.tok())
                (DVE.tensor_tensor(
                    out=xh[:], in0=B.X[:, dc * T + tt * 128:dc * T + tt * 128 + 128],
                    in1=ab_a(tc4)[:, off:off + 128], op=ALU.mult))
                (DVE.tensor_tensor(
                    out=xh[:], in0=xh[:], in1=ab_b(tc4)[:, off:off + 128], op=ALU.add))
                (DVE.tensor_scalar(
                    out=rin[:], in0=xh[:], scalar1=cons[:, dc:dc + 1],
                    scalar2=cons[:, 8 + dc:8 + dc + 1], op0=ALU.mult, op1=ALU.add))
                PE.wait(B.sDVE.tok())
                B.sPE.inc(PE.matmul(out=psr[:], lhsT=rin[:],
                                    rhs=rw[:, dc * 8:(dc + 1) * 8],
                                    start=(dc == 0), stop=False))
            PE.matmul(out=psr[:], lhsT=B.c_ones1b[0:1, :], rhs=rbhl[0:1, 0:8],
                      start=False, stop=False)
            B.sPE.inc(PE.matmul(out=psr[:], lhsT=B.c_ones1b[0:1, :],
                                rhs=rbhl[0:1, 8:16], start=False, stop=True))
            ACT.wait(B.sPE.tok())
            ACT.wait(B.sDVE.tok())
            (ACT.copy(out=lg[:], in_=psr[:]))
            DVE.wait(B.sACT.tok())
            mx = sc[:, 0:1]
            (DVE.reduce_max(out=mx, in_=lg[:], axis=AX))
            (DVE.tensor_scalar(out=mx, in0=mx, scalar1=-1.0, scalar2=None,
                                         op0=ALU.mult))
            ACT.wait(B.sDVE.tok())
            (ACT.activation(out=lg[:], in_=lg[:], func=AF.Exp, bias=mx))
            DVE.wait(B.sACT.tok())
            sm = sc[:, 1:2]
            (DVE.reduce_sum(out=sm, in_=lg[:], axis=AX))
            (DVE.reciprocal(out=sm, in_=sm))
            (DVE.tensor_scalar(out=lg[:], in0=lg[:], scalar1=sm,
                                         scalar2=None, op0=ALU.mult))
            m1 = sc[:, 2:3]
            m2 = sc[:, 3:4]
            (DVE.reduce_max(out=m1, in_=lg[:], axis=AX))
            (DVE.tensor_scalar(out=s8[:, 0:8], in0=lg[:], scalar1=m1,
                                         scalar2=-1e30, op0=ALU.is_ge, op1=ALU.mult))
            (DVE.tensor_tensor(out=s8[:, 8:16], in0=lg[:], in1=s8[:, 0:8],
                                         op=ALU.add))
            (DVE.reduce_max(out=m2, in_=s8[:, 8:16], axis=AX))
            (DVE.tensor_scalar(out=s8[:, 0:8], in0=lg[:], scalar1=m2,
                                         scalar2=None, op0=ALU.is_ge))
            (DVE.tensor_tensor(out=lg[:], in0=lg[:], in1=s8[:, 0:8],
                                         op=ALU.mult))
            sm2 = sc[:, 5:6]
            (DVE.reduce_sum(out=sm2, in_=lg[:], axis=AX))
            (DVE.reciprocal(out=sm2, in_=sm2))
            (DVE.tensor_scalar(out=lg[:], in0=lg[:], scalar1=sm2,
                                         scalar2=None, op0=ALU.mult))
            (DVE.tensor_copy(out=LGALL[:, tt * 8:(tt + 1) * 8], in_=lg[:]))
            (DVE.tensor_tensor(out=LGALL2[:, tt * 8:(tt + 1) * 8], in0=lg[:],
                               in1=LGALL[:, tt * 8:(tt + 1) * 8], op=ALU.subtract))
    B.barrier()

    w1d, w2d = B._dram["l0_w1"], B._dram["l0_w2"]
    for tg in range(NTG):
        with B.scope() as stg:
            hbuf = B.sb(stg, "l0_h", [128, NHC * TG], BF16)
            gacc = B.sb(stg, "l0_gacc", [128, NDC * TG])
            WB = B.sb(stg, "l0_wbt", [128, TG])
            wrow_sc = B.sb(stg, "l0_wrowsc", [1, 2 * TG], BF16, side="right")
            pswb = B.ps(stg, "l0_pswb", [128, TG])
            pswt = B.ps(stg, "l0_pswt", [1, 256], BF16)
            for e in range(8):
                B.barrier()
                for tti in range(TG // 128):
                    tt = tg * (TG // 128) + tti
                    PE.wait(B.sACT.tok())
                    PE.transpose(out=pswt[0:1, 0:128],
                                 in_=LGALL[:, tt * 8 + e:tt * 8 + e + 1],
                                 identity=B.c_id16[:])
                    B.sPE.inc(PE.transpose(out=pswt[0:1, 128:256],
                                           in_=LGALL2[:, tt * 8 + e:tt * 8 + e + 1],
                                           identity=B.c_id16[:]))
                    ACT.wait(B.sPE.tok())
                    (ACT.copy(out=wrow_sc[0:1, tti * 128:(tti + 1) * 128],
                                        in_=pswt[0:1, 0:128]))
                    (ACT.copy(out=wrow_sc[0:1, TG + tti * 128:TG + (tti + 1) * 128],
                                        in_=pswt[0:1, 128:256]))
                PE.wait(B.sACT.tok())
                B.bcast_hl(pswb[:], wrow_sc[0:1, 0:TG], wrow_sc[0:1, TG:2 * TG])
                ACT.wait(B.sPE.tok())
                (ACT.copy(out=WB[:], in_=pswb[:]))
                with B.scope() as ste:
                    ein = B.sb(ste, "l0_ein", [128, NDC * TG], BF16)
                    xh2 = B.sb(ste, "l0_xh2", [128, TG])
                    for dc in range(NDC):
                        (DVE.tensor_tensor(
                            out=xh2[:], in0=x_chunk(B, dc, tg), in1=ab_a(tg),
                            op=ALU.mult))
                        (DVE.tensor_tensor(
                            out=xh2[:], in0=xh2[:], in1=ab_b(tg), op=ALU.add))
                        (DVE.tensor_scalar(
                            out=ein[:, dc * TG:(dc + 1) * TG], in0=xh2[:],
                            scalar1=cons[:, 16 + e * 8 + dc:16 + e * 8 + dc + 1],
                            scalar2=cons[:, 80 + e * 8 + dc:80 + e * 8 + dc + 1],
                            op0=ALU.mult, op1=ALU.add))

                    def ev_h(hc, tcz, bank, pe_tok, e=e, hbuf=hbuf):
                        ACT.wait(pe_tok)
                        (ACT.activation(
                            out=hbuf[:, hc * TG:(hc + 1) * TG], in_=bank[:],
                            func=AF.Gelu,
                            bias=cons[:, 144 + e * 32 + hc:144 + e * 32 + hc + 1]))
                        return B.sACT.tok()

                    B.fm_proj(ste, "l0w1",
                              lambda hc: w1d[e, :, hc * 1024:(hc + 1) * 1024],
                              NDC, NHC,
                              lambda kc, tcz: ein[:, kc * TG:(kc + 1) * TG],
                              ev_h, ntc=1)
                with B.scope() as ste2:
                    ytmp = B.sb(ste2, "l0_ytmp", [128, TG])

                    def ev_y(oc, tcz, bank, pe_tok, e=e, gacc=gacc, ytmp=ytmp, WB=WB):
                        DVE.wait(pe_tok)
                        col = oc * TG
                        if e == 0:
                            (DVE.scalar_tensor_tensor(
                                out=gacc[:, col:col + TG], in0=bank[:],
                                scalar=cons[:, 400 + e * 8 + oc:400 + e * 8 + oc + 1],
                                in1=WB[:], op0=ALU.add, op1=ALU.mult))
                        else:
                            (DVE.scalar_tensor_tensor(
                                out=ytmp[:], in0=bank[:],
                                scalar=cons[:, 400 + e * 8 + oc:400 + e * 8 + oc + 1],
                                in1=WB[:], op0=ALU.add, op1=ALU.mult))
                            (DVE.tensor_tensor(
                                out=gacc[:, col:col + TG], in0=gacc[:, col:col + TG],
                                in1=ytmp[:], op=ALU.add))
                        return B.sDVE.tok()

                    B.fm_proj(ste2, "l0w2",
                              lambda oc: w2d[e, :, oc * 4096:(oc + 1) * 4096],
                              NHC, NDC,
                              lambda hc, tcz: hbuf[:, hc * TG:(hc + 1) * TG],
                              ev_y, ntc=1)
            B.barrier()
            with B.scope() as stf:
                gt = B.sb(stf, "l0_gt", [128, TG])
                for dc in range(NDC):
                    (ACT.activation(out=gt[:],
                                              in_=gacc[:, dc * TG:(dc + 1) * TG],
                                              func=AF.Silu))
                    DVE.wait(B.sACT.tok())
                    (DVE.tensor_tensor(
                        out=gt[:], in0=gt[:],
                        in1=B.S[:, dc * T + tg * TG:dc * T + tg * TG + TG],
                        op=ALU.mult))
                    (DVE.tensor_tensor(
                        out=x_chunk(B, dc, tg), in0=x_chunk(B, dc, tg), in1=gt[:],
                        op=ALU.add))
                    ACT.wait(B.sDVE.tok())
                B.barrier()
    B.barrier()


def emit_layer0(B):
    with B.scope() as lst:
        B.m1row = B.sb(lst, "l0_m1", [1, T], side="right")
        B.r1row = B.sb(lst, "l0_r1", [1, T], side="right")
        st_xn = B.scope()
        B.XN = B.sb(st_xn, "l0_xn", [128, NDC * T], BF16)

        def save_stats(tc, m, r):
            (B.DVE.tensor_copy(
                out=B.m1row[0:1, tc * TC:(tc + 1) * TC], in_=m))
            (B.DVE.tensor_copy(
                out=B.r1row[0:1, tc * TC:(tc + 1) * TC], in_=r))

        with B.scope() as st:
            emit_ln_stats(B, st, lambda dc, tc: x_chunk(B, dc, tc), False,
                          LN_EPS_OUTER, stats_cb=save_stats,
                          xn_out=B.XN, x32=lambda dc, tc: x_chunk(B, dc, tc),
                          name="l0ln")
        B.barrier()
        if DEV_STOP in ("ln", "stats"):
            for dc in range(NDC):
                (B.ACT.copy(out=B.S[:, dc * T:(dc + 1) * T], in_=B.XN[:, dc * T:(dc + 1) * T]))
            (B.ACT.copy(out=B.S[0:1, 0:T], in_=B.m1row[0:1, :]))
            (B.ACT.copy(out=B.S[0:1, T:2 * T], in_=B.r1row[0:1, :]))
            (B.ACT.copy(out=B.S[:, 2 * T + 100:2 * T + 101], in_=B.c_ones16[:, :]))
            B.barrier()
            st_xn.close()
            B.XN = None
            return
        emit_l0_attn(B, lst)
        if DEV_STOP == "attn":
            st_xn.close()
            B.XN = None
            return
        emit_l0_ffn(B, lst, st_xn)
        B.XN = None
    B.barrier()





def emit_router(B, st, rw_dram, nE, topk, lhs_tile, bias_hl_rows, LG_hi, LG_lo,
                renorm_eps=None, renorm=True, name="rt"):
    PE, DVE, ACT = B.PE, B.DVE, B.ACT
    rw = B.sb(st, name + "_rw", [128, NDC * nE], BF16)
    lg = B.sb(st, name + "_lg", [128, nE])
    s8 = B.sb(st, name + "_s8", [128, 2 * nE])
    sc = B.sb(st, name + "_sc", [128, 6])
    psr = B.ps(st, name + "_psr", [128, nE])
    B.dma(B.POOL, B.sDW, rw[:], rw_dram[:, :])
    B.barrier()
    for tt in range(T // 128):
        for dc in range(NDC):
            lt = lhs_tile(dc, tt)
            PE.wait(B.sDVE.tok())
            PE.wait(B.sACT.tok())
            B.sPE.inc(PE.matmul(out=psr[:], lhsT=lt, rhs=rw[:, dc * nE:(dc + 1) * nE],
                                start=(dc == 0), stop=False))
        if bias_hl_rows is not None:
            PE.matmul(out=psr[:], lhsT=B.c_ones1b[0:1, :], rhs=bias_hl_rows[0],
                      start=False, stop=False)
            B.sPE.inc(PE.matmul(out=psr[:], lhsT=B.c_ones1b[0:1, :],
                                rhs=bias_hl_rows[1], start=False, stop=True))
        else:
            B.sPE.inc(PE.matmul(out=psr[:], lhsT=B.c_ones1b[0:1, 0:nE],
                                rhs=B.c_zrow[0:1, 0:nE], start=False, stop=True))
        ACT.wait(B.sPE.tok())
        ACT.wait(B.sDVE.tok())
        (ACT.copy(out=lg[:], in_=psr[:]))
        DVE.wait(B.sACT.tok())
        mx = sc[:, 0:1]
        (DVE.reduce_max(out=mx, in_=lg[:], axis=AX))
        (DVE.tensor_scalar(out=mx, in0=mx, scalar1=-1.0, scalar2=None, op0=ALU.mult))
        ACT.wait(B.sDVE.tok())
        (ACT.activation(out=lg[:], in_=lg[:], func=AF.Exp, bias=mx))
        DVE.wait(B.sACT.tok())
        sm = sc[:, 1:2]
        (DVE.reduce_sum(out=sm, in_=lg[:], axis=AX))
        (DVE.reciprocal(out=sm, in_=sm))
        (DVE.tensor_scalar(out=lg[:], in0=lg[:], scalar1=sm, scalar2=None,
                           op0=ALU.mult))
        mcur = sc[:, 2:3]
        work = s8[:, 0:nE]
        msk = s8[:, nE:2 * nE]
        (DVE.tensor_copy(out=work, in_=lg[:]))
        for it in range(topk):
            (DVE.reduce_max(out=mcur, in_=work, axis=AX))
            if it < topk - 1:
                (DVE.tensor_scalar(out=msk, in0=work, scalar1=mcur, scalar2=-1e30,
                                   op0=ALU.is_ge, op1=ALU.mult))
                (DVE.tensor_tensor(out=work, in0=work, in1=msk, op=ALU.add))
        (DVE.tensor_scalar(out=msk, in0=lg[:], scalar1=mcur, scalar2=None,
                           op0=ALU.is_ge))
        (DVE.tensor_tensor(out=lg[:], in0=lg[:], in1=msk, op=ALU.mult))
        if renorm:
            sm2 = sc[:, 3:4]
            (DVE.reduce_sum(out=sm2, in_=lg[:], axis=AX))
            if renorm_eps:
                (DVE.tensor_scalar(out=sm2, in0=sm2, scalar1=renorm_eps,
                                   scalar2=None, op0=ALU.add))
            (DVE.reciprocal(out=sm2, in_=sm2))
            (DVE.tensor_scalar(out=lg[:], in0=lg[:], scalar1=sm2, scalar2=None,
                               op0=ALU.mult))
        (DVE.tensor_copy(out=LG_hi[:, tt * nE:(tt + 1) * nE], in_=lg[:]))
        (DVE.tensor_tensor(out=LG_lo[:, tt * nE:(tt + 1) * nE], in0=lg[:],
                           in1=LG_hi[:, tt * nE:(tt + 1) * nE], op=ALU.subtract))
    B.barrier()


def build_wb(B, e, tg, TG, nE, LG_hi, LG_lo, wrow_sc, pswt, pswb, WB):
    PE, ACT = B.PE, B.ACT
    for tti in range(TG // 128):
        tt = (tg * TG) // 128 + tti
        PE.wait(B.sACT.tok())
        PE.wait(B.sDVE.tok())
        PE.transpose(out=pswt[0:1, 0:128], in_=LG_hi[:, tt * nE + e:tt * nE + e + 1],
                     identity=B.c_id16[:])
        B.sPE.inc(PE.transpose(out=pswt[0:1, 128:256],
                               in_=LG_lo[:, tt * nE + e:tt * nE + e + 1],
                               identity=B.c_id16[:]))
        ACT.wait(B.sPE.tok())
        (ACT.copy(out=wrow_sc[0:1, tti * 128:(tti + 1) * 128], in_=pswt[0:1, 0:128]))
        (ACT.copy(out=wrow_sc[0:1, TG + tti * 128:TG + (tti + 1) * 128],
                  in_=pswt[0:1, 128:256]))
    PE.wait(B.sACT.tok())
    B.bcast_hl(pswb[0:128, 0:TG], wrow_sc[0:1, 0:TG], wrow_sc[0:1, TG:2 * TG])
    ACT.wait(B.sPE.tok())
    (ACT.copy(out=WB[:], in_=pswb[0:128, 0:TG]))
    return B.sACT.tok()



L1_EXP_BIAS = -60.0


def make_l1_atlas(core):
    import ml_dtypes as _md
    q_off = (core % 2) * T
    p = np.arange(128)[:, None]
    u = np.arange(6016)[None, :]
    return np.ascontiguousarray(((p - u) <= (q_off - 3968)).astype(_md.bfloat16))


def l1_atlas_off(qc, kt):
    d0 = qc * TC - kt * 128
    return d0 + 3968


def host_prep_l1(B, inp):
    B.din("l1_wqkv", pack_km(inp["L1_attn_w_qkv"]))
    bq = np.asarray(inp["L1_attn_b_qkv"]).astype(np.float32)
    B.din("l1_batt", np.concatenate(
        [pack_bias_part(bq[0:1024]), pack_bias_part(bq[1024:2048]),
         pack_bias_part(bq[2048:3072]), pack_bias_part(inp["L1_attn_b_out"])],
        axis=1))
    B.din("l1_wout", pack_km(inp["L1_attn_w_out"]))
    B.din("l1_atlas", [make_l1_atlas(c) for c in range(NCORE)], BF16)
    W1 = np.asarray(inp["L1_ffn_W1"]); W3 = np.asarray(inp["L1_ffn_W3"])
    W2 = np.asarray(inp["L1_ffn_W2"])
    B.din("l1_w1", np.stack([pack_km(W1[e]) for e in range(8)]))
    B.din("l1_w3", np.stack([pack_km(W3[e]) for e in range(8)]))
    B.din("l1_w2", np.stack([pack_km(W2[e]) for e in range(8)]))
    b1 = np.asarray(inp["L1_ffn_b1"]); b3 = np.asarray(inp["L1_ffn_b3"])
    b2 = np.asarray(inp["L1_ffn_b2"])
    B.din("l1_bffn", np.concatenate(
        [pack_bias_part(b1[e]) for e in range(8)]
        + [pack_bias_part(b3[e]) for e in range(8)]
        + [pack_bias_part(b2[e]) for e in range(8)]
        + [pack_bias_part(inp["L1_ffn_bf"])], axis=1))
    B.din("l1_wg", pack_km(inp["L1_ffn_Wg"], mtile=8))
    B.din("l1_bg", np.asarray(inp["L1_ffn_bg"]).reshape(1, 8).astype(np.float32))
    B.din("l1_wf", pack_km(inp["L1_ffn_Wf"]))


def emit_l1_attn(B, lst):
    PE, DVE, ACT, POOL, SP = B.PE, B.DVE, B.ACT, B.POOL, B.SP
    NKT = 2 * T // 128
    wqkv = B._dram["l1_wqkv"]
    batt = B.sb(lst, "l1_batt_sb", [128, 32], side="right")
    B.dma(SP, B.sDX, batt[:], B._dram["l1_batt"][:, :])
    atlas = B.sb(lst, "l1_atlas_sb", [128, 6016], BF16, side="right")
    B.dma(SP, B.sDX, atlas[:], B._dram["l1_atlas"][:, :])
    k_in = B.dint("l1_k_in", [1024, T], BF16)
    k_out = B.dint("l1_k_out", [2048, T], BF16)
    v_in = B.dint("l1_vagi", [T, 1024], BF16)
    v_out = B.dint("l1_vago", [2 * T, 1024], BF16)
    scc1 = B.cc_sem()
    scc2 = B.cc_sem()

    with B.scope() as st:
        KF = B.sb(st, "l1_kf", [128, NDC * T], BF16)

        def ev_k(oc, tc, bank, pe_tok):
            ACT.wait(pe_tok)
            col = oc * T + tc * TC
            (ACT.activation(out=KF[:, col:col + TC], in_=bank[:], func=AF.Identity,
                            bias=batt[:, 8 + oc:8 + oc + 1]))
            return B.sACT.tok()

        with B.scope() as stq:
            B.fm_proj(stq, "l1k",
                      lambda oc: wqkv[:, (8 + oc) * 1024:(9 + oc) * 1024],
                      NDC, NDC, lambda kc, tc: xn_chunk(B, kc, tc), ev_k)
        POOL.wait(B.sACT.tok())
        B.dma(POOL, B.sDW, k_in.rearrange("(dc p) t -> p dc t", p=128),
              KF[:].rearrange("p (dc t) -> p dc t", dc=NDC))
        POOL.wait(B.sDW.tok())
        scc1.inc(POOL.collective_compute(
            "AllGather", ALU.bypass, replica_groups=PAIRS,
            ins=[k_in[:, :]], outs=[k_out[:, :]]))
    B.barrier()
    with B.scope() as st:
        VT = B.sb(st, "l1_vt", [128, (T // 128) * 1024], BF16)
        VFO = B.sb(st, "l1_vfo", [128, T], BF16)
        pst = [B.ps(st, f"l1_pst{i}", [128, 64], BF16) for i in range(2)]
        with B.scope() as stv:
            psv = [B.ps(stv, f"l1_psv{i}", [128, TC]) for i in range(2)]
            wbv = [B.sb(stv, f"l1_wbv{i}", [128, 1024], BF16) for i in range(2)]
            wsem = [B.sDW, B.sDW2]
            wtok = {}
            B.barrier()
            for pre in range(2):
                wtok[pre] = B.dma(POOL, wsem[pre], wbv[pre][:],
                                  wqkv[:, (16 + pre) * 1024:(17 + pre) * 1024])
            for oc in range(NDC):
                slot = oc % 2
                for tc in range(NTC):
                    bank = tc % 2
                    PE.wait(wtok[oc])
                    PE.wait(B.sACT.tok())
                    tiles = [(wbv[slot][:, kc * 128:(kc + 1) * 128],
                              xn_chunk(B, kc, tc)) for kc in range(NDC)]
                    B.mm_acc(psv[bank][:], tiles, sem=B.sPE)
                    ACT.wait(B.sPE.tok())
                    (ACT.activation(out=VFO[:, tc * TC:(tc + 1) * TC],
                                    in_=psv[bank][:], func=AF.Identity,
                                    bias=batt[:, 16 + oc:16 + oc + 1]))
                for sub in range(2):
                    h = oc * 2 + sub
                    p0 = sub * 64
                    for tt in range(T // 128):
                        par = tt % 2
                        PE.wait(B.sACT.tok())
                        B.sPE.inc(PE.transpose(
                            out=pst[par][:, 0:64],
                            in_=VFO[p0:p0 + 64, tt * 128:(tt + 1) * 128],
                            identity=B.c_id64s[p0:p0 + 64, 0:64]))
                        ACT.wait(B.sPE.tok())
                        (ACT.copy(out=VT[:, tt * 1024 + h * 64:tt * 1024 + h * 64 + 64],
                                  in_=pst[par][:, 0:64]))
                if oc + 2 < NDC:
                    POOL.wait(B.sPE.tok())
                    wtok[oc + 2] = B.dma(POOL, wsem[slot], wbv[slot][:],
                                         wqkv[:, (18 + oc) * 1024:(19 + oc) * 1024])
        POOL.wait(B.sACT.tok())
        B.dma(POOL, B.sDW2, v_in.rearrange("(tt p) e -> p tt e", p=128),
              VT[:].rearrange("p (tt e) -> p tt e", e=1024))
        POOL.wait(B.sDW2.tok())
        scc2.inc(POOL.collective_compute(
            "AllGather", ALU.bypass, replica_groups=PAIRS,
            ins=[v_in[:, :]], outs=[v_out[:, :]]))
        POOL.wait(scc1.tok())
        POOL.wait(scc2.tok())
    B.barrier()

    st_qot = B.scope()
    QOT = B.sb(st_qot, "l1_qot", [128, NDC * T], BF16, side="right")

    def ev_q(oc, tc, bank, pe_tok):
        ACT.wait(pe_tok)
        col = oc * T + tc * TC
        (ACT.activation(out=QOT[:, col:col + TC], in_=bank[:], func=AF.Identity,
                        bias=batt[:, oc:oc + 1]))
        return B.sACT.tok()

    with B.scope() as stq:
        B.fm_proj(stq, "l1q", lambda oc: wqkv[:, oc * 1024:(oc + 1) * 1024],
                  NDC, NDC, lambda kc, tc: xn_chunk(B, kc, tc), ev_q)
    B.barrier()
    B.close_xn()

    NKT = 2 * T // 128
    with B.scope() as st:
        VTH = B.sb(st, "l1_vth", [128, NKT * 256], BF16)
        KH = B.sb(st, "l1_kh", [128, 2 * T], BF16)
        KSQ = B.sb(st, "l1_ksq", [64, 2 * T], BF16)
        K2C = B.sb(st, "l1_k2c", [128, NKT])
        NB = 3
        BQ = B.sb(st, "l1_bq", [128, TC])
        t1 = [B.sb(st, f"l1_t1_{i}", [128, TC]) for i in range(NB)]
        tu = [B.sb(st, f"l1_tu_{i}", [128, TC]) for i in range(NB)]
        tr = [B.sb(st, f"l1_tr_{i}", [128, TC]) for i in range(NB)]
        tq = [B.sb(st, f"l1_tq_{i}", [128, TC]) for i in range(NB)]
        PM = [B.sb(st, f"l1_pm_{i}", [128, TC], BF16) for i in range(NB)]
        ZB = tq[0][0:64, :]
        q2sq = B.sb(st, "l1_q2sq", [128, TC], BF16)
        rows = B.sb(st, "l1_rows", [1, TC], side="right")
        rhl = B.sb(st, "l1_rhl", [1, 2 * TC], BF16, side="right")
        psD = [B.ps(st, f"l1_psD{i}", [128, TC]) for i in range(NB)]
        psO = [B.ps(st, "l1_psO", [128, TC])]
        psden = [B.ps(st, "l1_psden", [1, TC])]
        psz = B.ps(st, "l1_psz", [128, TC])
        psk2 = psz
        psq2 = psden[0]
        for vq in range(4):
            B.barrier()
            B.dma(SP, B.sDX,
                  VTH[:].rearrange("p (k e) -> p k e", e=256),
                  v_out[:, vq * 256:(vq + 1) * 256].rearrange(
                      "(k p) e -> p k e", p=128))
            for hp in range(2):
                h0 = vq * 4 + hp * 2
                B.barrier()
                for sub in range(2):
                    hh = h0 + sub
                    B.dma(SP, B.sDX, KH[sub * 64:sub * 64 + 64, 0:T],
                          k_out[hh * 64:hh * 64 + 64, :])
                    B.dma(SP, B.sDX, KH[sub * 64:sub * 64 + 64, T:2 * T],
                          k_out[1024 + hh * 64:1024 + hh * 64 + 64, :])
                ACT.wait(B.sDX.tok())
                for sub in range(2):
                    h = h0 + sub
                    p0 = sub * 64
                    oc_l = h // 2
                    PE.wait(B.sACT.tok())
                    (ACT.activation(out=KSQ[0:64, :], in_=KH[p0:p0 + 64, :],
                                    func=AF.Square))
                    PE.wait(B.sACT.tok())
                    PE.wait(B.sDVE.tok())
                    for kt in range(NKT):
                        B.sPE.inc(PE.matmul(out=psk2[0:128, kt:kt + 1],
                                            lhsT=KSQ[0:64, kt * 128:(kt + 1) * 128],
                                            rhs=B.c_ones16[0:64, 0:1],
                                            start=True, stop=True))
                    ACT.wait(B.sPE.tok())
                    (ACT.copy(out=K2C[:], in_=psk2[0:128, 0:NKT]))
                    for qc in range(NTC):
                        qcol = oc_l * T + qc * TC
                        ACT.wait(B.sPE.tok())
                        ACT.wait(B.sDVE.tok())
                        (ACT.activation(out=q2sq[p0:p0 + 64, :],
                                        in_=QOT[p0:p0 + 64, qcol:qcol + TC],
                                        func=AF.Square))
                        PE.wait(B.sACT.tok())
                        B.sPE.inc(PE.matmul(out=psq2[:], lhsT=B.c_ones16[p0:p0 + 64, :],
                                            rhs=q2sq[p0:p0 + 64, :], start=True, stop=True))
                        DVE.wait(B.sPE.tok())
                        m = rows[0:1, 0:TC]
                        (DVE.tensor_scalar(out=m, in0=psq2[0:1, :], scalar1=1e-6,
                                           scalar2=None, op0=ALU.add))
                        B.split_hl(m, rhl[0:1, 0:TC], rhl[0:1, TC:2 * TC])
                        PE.wait(B.sDVE.tok())
                        PE.wait(B.sACT.tok())
                        B.bcast_hl(psz[:], rhl[0:1, 0:TC], rhl[0:1, TC:2 * TC])
                        ACT.wait(B.sPE.tok())
                        (ACT.copy(out=BQ[:], in_=psz[:]))
                        ob = 0
                        kts = [kt for kt in range(NKT) if kt <= 4 * qc + 20]
                        n_kt = len(kts)
                        freeD = getattr(B, "_l1fD", [None] * NB)
                        freeP = getattr(B, "_l1fP", [None] * NB)
                        B._l1fD, B._l1fP = freeD, freeP
                        hl = h % 4
                        petoks = {}

                        def emit_mm(j):
                            bj = j % NB
                            PE.wait(freeD[bj])
                            B.sPE.inc(PE.matmul(
                                out=psD[bj][:],
                                lhsT=KH[p0:p0 + 64, kts[j] * 128:(kts[j] + 1) * 128],
                                rhs=QOT[p0:p0 + 64, qcol:qcol + TC],
                                start=True, stop=True))
                            petoks[j] = B.sPE.tok()

                        emit_mm(0)
                        if n_kt > 1:
                            emit_mm(1)
                        for ki, kt in enumerate(kts):
                            bank = ki % NB
                            first, last = (ki == 0), (ki == n_kt - 1)
                            DVE.wait(petoks[ki])
                            (DVE.tensor_scalar(out=t1[bank][:], in0=psD[bank][:],
                                               scalar1=-2.0, scalar2=K2C[:, kt:kt + 1],
                                               op0=ALU.mult, op1=ALU.add))
                            ACT.wait(B.sDVE.tok())
                            (ACT.activation(out=tq[bank][:], in_=psD[bank][:],
                                            func=AF.Square))
                            freeD[bank] = B.sACT.tok()
                            (DVE.tensor_tensor(out=tu[bank][:], in0=t1[bank][:],
                                               in1=BQ[:], op=ALU.add))
                            ACT.wait(B.sDVE.tok())
                            (ACT.activation(out=t1[bank][:], in_=tu[bank][:],
                                            func=AF.Ln))
                            (ACT.activation(out=tr[bank][:], in_=t1[bank][:],
                                            func=AF.Exp, scale=-1.0))
                            POOL.wait(B.sACT.tok())
                            (POOL.tensor_tensor(out=tu[bank][:], in0=tq[bank][:],
                                                in1=tr[bank][:], op=ALU.mult))
                            ACT.wait(B.sPOOL.tok())
                            ACT.wait(freeP[bank])
                            (ACT.activation(out=PM[bank][:], in_=tu[bank][:],
                                            func=AF.Exp, bias=L1_EXP_BIAS))
                            if kt >= 4 * qc:
                                DVE.wait(B.sACT.tok())
                                ao = l1_atlas_off(qc, kt)
                                (DVE.tensor_tensor(out=PM[bank][:], in0=PM[bank][:],
                                                   in1=atlas[:, ao:ao + TC],
                                                   op=ALU.mult))
                                ptok = B.sDVE.tok()
                            else:
                                ptok = B.sACT.tok()
                            if ki + 2 < n_kt:
                                emit_mm(ki + 2)
                            PE.wait(ptok)
                            B.sPE.inc(PE.matmul(
                                out=psO[ob][0:64, :],
                                lhsT=VTH[:, kt * 256 + hl * 64:kt * 256 + hl * 64 + 64],
                                rhs=PM[bank][:], start=first, stop=last))
                            B.sPE.inc(PE.matmul(
                                out=psden[ob][:], lhsT=B.c_ones16[:, 0:1],
                                rhs=PM[bank][:], start=first, stop=last))
                            freeP[bank] = B.sPE.tok()
                        DVE.wait(B.sPE.tok())
                        den = rows[0:1, 0:TC]
                        (DVE.reciprocal(out=den, in_=psden[ob][0:1, :]))
                        B.split_hl(den, rhl[0:1, 0:TC], rhl[0:1, TC:2 * TC])
                        PE.wait(B.sDVE.tok())
                        B.bcast_hl(psz[0:64, :], rhl[0:1, 0:TC], rhl[0:1, TC:2 * TC],
                                   parts=64)
                        ACT.wait(B.sPE.tok())
                        (ACT.copy(out=ZB, in_=psz[0:64, :]))
                        DVE.wait(B.sACT.tok())
                        (DVE.tensor_tensor(out=QOT[p0:p0 + 64, qcol:qcol + TC],
                                           in0=psO[ob][0:64, :], in1=ZB,
                                           op=ALU.mult))
    wout = B._dram["l1_wout"]
    B.barrier()

    def ev_s(oc, tc, bank, pe_tok):
        ACT.wait(pe_tok)
        col = oc * T + tc * TC
        (ACT.activation(out=B.S[:, col:col + TC], in_=bank[:],
                        func=AF.Identity, bias=batt[:, 24 + oc:24 + oc + 1]))
        return B.sACT.tok()

    with B.scope() as sto:
        B.fm_proj(sto, "l1o", lambda oc: wout[:, oc * 1024:(oc + 1) * 1024],
                  NDC, NDC,
                  lambda kc, tc: QOT[:, kc * T + tc * TC:kc * T + tc * TC + TC],
                  ev_s)
    B.barrier()
    st_qot.close()


def emit_l1_ffn(B, lst):
    PE, DVE, ACT, POOL = B.PE, B.DVE, B.ACT, B.POOL
    NHC = 16
    TG = 512
    NTG = T // TG
    bffn = B.sb(lst, "l1_bffn_sb", [128, 328], side="right")
    B.dma(B.SP, B.sDX, bffn[:], B._dram["l1_bffn"][:, :])
    LGH = B.sb(lst, "l1_lgh", [128, 8 * (T // 128)], BF16, side="right")
    LGL = B.sb(lst, "l1_lgl", [128, 8 * (T // 128)], BF16, side="right")
    bgrow = B.sb(lst, "l1_bgrow", [1, 8], side="right")
    bghl = B.sb(lst, "l1_bghl", [1, 16], BF16, side="right")
    B.dma(B.SP, B.sDX, bgrow[:], B._dram["l1_bg"][:, :])
    B.barrier()
    B.split_hl(bgrow[0:1, :], bghl[0:1, 0:8], bghl[0:1, 8:16])
    B.barrier()
    with B.scope() as st:
        emit_router(B, st, B._dram["l1_wg"], 8, 2,
                    lambda dc, tt: B.XN[:, dc * T + tt * 128:dc * T + tt * 128 + 128],
                    (bghl[0:1, 0:8], bghl[0:1, 8:16]), LGH, LGL,
                    renorm=False, name="l1rt")
    w1d, w3d, w2d = B._dram["l1_w1"], B._dram["l1_w3"], B._dram["l1_w2"]
    wfd = B._dram["l1_wf"]
    for tg in range(NTG):
        with B.scope() as stg:
            hbuf = B.sb(stg, "l1_h", [128, NHC * TG], BF16)
            h1t = [B.sb(stg, f"l1_h1t{i}", [128, TG]) for i in range(2)]
            gacc = B.sb(stg, "l1_gacc", [128, NDC * TG])
            WB = B.sb(stg, "l1_wbt", [128, TG])
            ytmp = B.sb(stg, "l1_ytmp", [128, TG])
            wrow_sc = B.sb(stg, "l1_wrsc", [1, 2 * TG], BF16, side="right")
            pswb = B.ps(stg, "l1_pswb", [128, TG])
            pswt = B.ps(stg, "l1_pswt", [1, 256], BF16)
            ps3 = [B.ps(stg, f"l1_ps3{i}", [128, TG]) for i in range(2)]
            for e in range(8):
                B.barrier()
                build_wb(B, e, tg, TG, 8, LGH, LGL, wrow_sc, pswt, pswb, WB)
                with B.scope() as ste:
                    wb1 = [B.sb(ste, f"l1_wb1{i}", [128, 1024], BF16) for i in range(2)]
                    wb3 = [B.sb(ste, f"l1_wb3{i}", [128, 1024], BF16) for i in range(2)]
                    wsem = [B.sDW, B.sDW2]
                    wtok = {}
                    B.barrier()
                    for pre in range(2):
                        B.dma(POOL, wsem[pre], wb1[pre][:],
                              w1d[e, :, pre * 1024:(pre + 1) * 1024])
                        wtok[pre] = B.dma(POOL, wsem[pre], wb3[pre][:],
                                          w3d[e, :, pre * 1024:(pre + 1) * 1024])
                    for hc in range(NHC):
                        slot = hc % 2
                        bank = hc % 2
                        PE.wait(wtok[hc])
                        PE.wait(B.sACT.tok())
                        PE.wait(B.sDVE.tok())
                        tiles = [(wb1[slot][:, kc * 128:(kc + 1) * 128],
                                  xn_chunk(B, kc, tg)) for kc in range(NDC)]
                        B.mm_acc(ps3[bank][:], tiles, sem=B.sPE)
                        ACT.wait(B.sPE.tok())
                        (ACT.activation(out=h1t[bank][:], in_=ps3[bank][:],
                                        func=AF.Identity,
                                        bias=bffn[:, e * 16 + hc:e * 16 + hc + 1]))
                        PE.wait(B.sACT.tok())
                        tiles = [(wb3[slot][:, kc * 128:(kc + 1) * 128],
                                  xn_chunk(B, kc, tg)) for kc in range(NDC)]
                        B.mm_acc(ps3[bank][:], tiles, sem=B.sPE)
                        DVE.wait(B.sPE.tok())
                        (DVE.scalar_tensor_tensor(
                            out=h1t[bank][:], in0=ps3[bank][:],
                            scalar=bffn[:, 128 + e * 16 + hc:128 + e * 16 + hc + 1],
                            in1=h1t[bank][:], op0=ALU.add, op1=ALU.mult))
                        ACT.wait(B.sDVE.tok())
                        (ACT.activation(out=hbuf[:, hc * TG:(hc + 1) * TG],
                                        in_=h1t[bank][:], func=AF.Silu))
                        if hc + 2 < NHC:
                            POOL.wait(B.sPE.tok())
                            B.dma(POOL, wsem[slot], wb1[slot][:],
                                  w1d[e, :, (hc + 2) * 1024:(hc + 3) * 1024])
                            wtok[hc + 2] = B.dma(POOL, wsem[slot], wb3[slot][:],
                                                 w3d[e, :, (hc + 2) * 1024:(hc + 3) * 1024])
                with B.scope() as ste2:
                    def ev_y(oc, tcz, bank, pe_tok, e=e):
                        DVE.wait(pe_tok)
                        col = oc * TG
                        if e == 0:
                            (DVE.scalar_tensor_tensor(
                                out=gacc[:, col:col + TG], in0=bank[:],
                                scalar=bffn[:, 256 + e * 8 + oc:256 + e * 8 + oc + 1],
                                in1=WB[:], op0=ALU.add, op1=ALU.mult))
                        else:
                            (DVE.scalar_tensor_tensor(
                                out=ytmp[:], in0=bank[:],
                                scalar=bffn[:, 256 + e * 8 + oc:256 + e * 8 + oc + 1],
                                in1=WB[:], op0=ALU.add, op1=ALU.mult))
                            (DVE.tensor_tensor(out=gacc[:, col:col + TG],
                                               in0=gacc[:, col:col + TG], in1=ytmp[:],
                                               op=ALU.add))
                        return B.sDVE.tok()

                    B.fm_proj(ste2, "l1w2",
                              lambda oc, e=e: w2d[e, :, oc * 2048:(oc + 1) * 2048],
                              NHC, NDC,
                              lambda hc, tcz: hbuf[:, hc * TG:(hc + 1) * TG],
                              ev_y, ntc=1)
            B.barrier()
            with B.scope() as stf:
                mb = B.sb(stf, "l1_mb", [128, NDC * TG], BF16)
                gt2 = B.sb(stf, "l1_gt2", [128, TG])
                for dc in range(NDC):
                    (DVE.tensor_copy(out=mb[:, dc * TG:(dc + 1) * TG],
                                     in_=gacc[:, dc * TG:(dc + 1) * TG]))
                B.barrier()

                def ev_g(oc, tcz, bank, pe_tok, tg=tg):
                    ACT.wait(pe_tok)
                    ACT.wait(B.sDVE.tok())
                    (ACT.activation(out=ytmp[:], in_=bank[:], func=AF.Silu,
                                    bias=bffn[:, 320 + oc:320 + oc + 1]))
                    DVE.wait(B.sACT.tok())
                    col = oc * T + tg * TG
                    (DVE.tensor_tensor(out=gt2[:], in0=ytmp[:],
                                       in1=B.S[:, col:col + TG], op=ALU.mult))
                    (DVE.tensor_tensor(out=B.X[:, col:col + TG],
                                       in0=B.X[:, col:col + TG], in1=gt2[:],
                                       op=ALU.add))
                    return B.sDVE.tok()

                B.fm_proj(stf, "l1wf",
                          lambda oc: wfd[:, oc * 1024:(oc + 1) * 1024],
                          NDC, NDC,
                          lambda kc, tcz: mb[:, kc * TG:(kc + 1) * TG],
                          ev_g, ntc=1)
    B.barrier()


def emit_layer1(B):
    with B.scope() as lst:
        st_xn = B.scope()
        B.XN = B.sb(st_xn, "l1_xn", [128, NDC * T], BF16)

        def close_xn():
            st_xn.close()
            B.XN = None
        B.close_xn = close_xn
        with B.scope() as st:
            emit_ln_stats(B, st, lambda dc, tc: x_chunk(B, dc, tc), False,
                          LN_EPS_OUTER, xn_out=B.XN,
                          x32=lambda dc, tc: x_chunk(B, dc, tc), name="l1ln")
        B.barrier()
        emit_l1_attn(B, lst)
        st_xn2 = B.scope()
        B.XN = B.sb(st_xn2, "l1_xn2", [128, NDC * T], BF16)
        with B.scope() as st:
            emit_ln_stats(B, st, lambda dc, tc: x_chunk(B, dc, tc), False,
                          LN_EPS_OUTER, xn_out=B.XN,
                          x32=lambda dc, tc: x_chunk(B, dc, tc), name="l1ln2")
        B.barrier()
        emit_l1_ffn(B, lst)
        st_xn2.close()
        B.XN = None
    B.barrier()





L2_SCALE = 1.0 / float(np.sqrt(160.0))


def host_prep_l2(B, inp):
    aug = lambda w, b: np.ascontiguousarray(
        np.vstack([np.asarray(w), np.asarray(b).reshape(1, -1)]).astype(np.float32))
    B.din("l2_wdq", pack_km(inp["L2_attn_W_DQ"], mtile=32))
    B.din("l2_bdq", np.asarray(inp["L2_attn_b_DQ"]).reshape(32, 1).astype(np.float32))
    B.din("l2_wdkv", pack_km(inp["L2_attn_W_DKV"], mtile=32))
    B.din("l2_bdkv", np.asarray(inp["L2_attn_b_DKV"]).reshape(32, 1).astype(np.float32))
    B.din("l2_wkr", pack_km(inp["L2_attn_W_KR"], mtile=32))
    B.din("l2_bkr", np.asarray(inp["L2_attn_b_KR"]).reshape(32, 1).astype(np.float32))
    B.din("l2_wuqa", aug(inp["L2_attn_W_UQ"], inp["L2_attn_b_UQ"]))
    B.din("l2_wuka", aug(inp["L2_attn_W_UK"], inp["L2_attn_b_UK"]))
    B.din("l2_wuva", aug(inp["L2_attn_W_UV"], inp["L2_attn_b_UV"]))
    B.din("l2_wqra", aug(inp["L2_attn_W_QR"], inp["L2_attn_b_QR"]))
    B.din("l2_wo", pack_km(inp["L2_attn_W_O"]))
    B.din("l2_bo", pack_bias_part(inp["L2_attn_b_O"]))
    B.din("l2_gw", pack_km(tp(inp["L2_ffn_gate_w"]), mtile=8))
    B.din("l2_geb", np.asarray(inp["L2_ffn_expert_bias"]).reshape(1, 8).astype(np.float32))
    wu = np.asarray(inp["L2_ffn_w_up"]); wd = np.asarray(inp["L2_ffn_w_down"])
    B.din("l2_wup", np.stack([pack_km(tp(wu[e])) for e in range(8)]))
    B.din("l2_wdn", np.stack([pack_km(tp(wd[e])) for e in range(8)]))


def emit_l2_attn(B, lst):
    PE, DVE, ACT, POOL, SP = B.PE, B.DVE, B.ACT, B.POOL, B.SP
    NKT = 2 * T // 128
    ag_in = B.dint("l2_ag_in", [64, T], BF16)
    ag_out = B.dint("l2_ag_out", [128, T], BF16)
    scc = B.cc_sem()
    st_att = B.scope()
    batt = B.sb(st_att, "l2_batt", [32, 3], side="right")
    B.dma(SP, B.sDX, batt[0:32, 0:1], B._dram["l2_bdq"][:, :])
    B.dma(SP, B.sDX, batt[0:32, 1:2], B._dram["l2_bdkv"][:, :])
    B.dma(SP, B.sDX, batt[0:32, 2:3], B._dram["l2_bkr"][:, :])
    CQ = B.sb(st_att, "l2_cq", [33, T], BF16, side="right")
    CKVF = B.sb(st_att, "l2_ckvf", [33, 2 * T], BF16, side="right")
    KRF = B.sb(st_att, "l2_krf", [32, 2 * T], BF16, side="right")
    WUQ = B.sb(st_att, "l2_wuq", [33, 1024], BF16, side="right")
    WUK = B.sb(st_att, "l2_wuk", [33, 1024], BF16, side="right")
    WUV = B.sb(st_att, "l2_wuv", [33, 1024], BF16, side="right")
    WQR = B.sb(st_att, "l2_wqr", [33, 256], BF16, side="right")
    B.dma(POOL, B.sDW, WUQ[:], B._dram["l2_wuqa"][:, :])
    B.dma(POOL, B.sDW, WUK[:], B._dram["l2_wuka"][:, :])
    B.dma(POOL, B.sDW, WUV[:], B._dram["l2_wuva"][:, :])
    B.dma(POOL, B.sDW, WQR[:], B._dram["l2_wqra"][:, :])

    with B.scope() as st:
        wb = [B.sb(st, f"l2_wlr{i}", [128, 8 * 32], BF16) for i in range(3)]
        CKO = B.sb(st, "l2_cko", [64, T], BF16)
        psp = [B.ps(st, f"l2_psp{i}", [32, TC]) for i in range(2)]
        B.barrier()
        (DVE.memset(CQ[32:33, :], 1.0))
        (DVE.memset(CKVF[32:33, :], 1.0))
        B.dma(POOL, B.sDW, wb[0][:], B._dram["l2_wdq"][:, :])
        B.dma(POOL, B.sDW, wb[1][:], B._dram["l2_wdkv"][:, :])
        B.dma(POOL, B.sDW, wb[2][:], B._dram["l2_wkr"][:, :])
        B.barrier()
        for which, (wbx, bcol, dst) in enumerate(
                [(wb[0], 0, None), (wb[1], 1, None), (wb[2], 2, None)]):
            for tc in range(NTC):
                bank = tc % 2
                PE.wait(B.sACT.tok())
                tiles = [(wbx[:, kc * 32:(kc + 1) * 32], xn_chunk(B, kc, tc))
                         for kc in range(NDC)]
                B.mm_acc(psp[bank][0:32, :], tiles, sem=B.sPE)
                ACT.wait(B.sPE.tok())
                if which == 0:
                    outap = CQ[0:32, tc * TC:(tc + 1) * TC]
                elif which == 1:
                    outap = CKO[0:32, tc * TC:(tc + 1) * TC]
                else:
                    outap = CKO[32:64, tc * TC:(tc + 1) * TC]
                (ACT.activation(out=outap, in_=psp[bank][0:32, :], func=AF.Identity,
                                bias=batt[0:32, which:which + 1]))
        POOL.wait(B.sACT.tok())
        B.dma(POOL, B.sDW, ag_in[:, :], CKO[:, :])
        POOL.wait(B.sDW.tok())
        scc.inc(POOL.collective_compute(
            "AllGather", ALU.bypass, replica_groups=PAIRS,
            ins=[ag_in[:, :]], outs=[ag_out[:, :]]))
        POOL.wait(scc.tok())
        SP.wait(scc.tok())
        for r in range(2):
            B.dma(SP, B.sDX, CKVF[0:32, r * T:(r + 1) * T],
                  ag_out[r * 64:r * 64 + 32, :])
            B.dma(SP, B.sDX, KRF[0:32, r * T:(r + 1) * T],
                  ag_out[r * 64 + 32:r * 64 + 64, :])
    B.barrier()
    B.close_xn()

    OT = B.sb(st_att, "l2_ot", [128, NDC * T], BF16, side="right")
    with B.scope() as st:
        QC = B.sb(st, "l2_qc", [128, T], BF16)
        QR = B.sb(st, "l2_qr", [32, T], BF16)
        KC = B.sb(st, "l2_kc", [128, 2 * T], BF16)
        VTM = B.sb(st, "l2_vtm", [128, NKT * 128], BF16)
        PM = [B.sb(st, f"l2_pm{i}", [128, TC], BF16) for i in range(4)]
        ZB = B.sb(st, "l2_zb", [128, TC])
        rows = B.sb(st, "l2_rows", [1, TC], side="right")
        rhl = B.sb(st, "l2_rhl", [1, 2 * TC], BF16, side="right")
        psS = [B.ps(st, f"l2_psS{i}", [128, TC]) for i in range(4)]
        psO = B.ps(st, "l2_psO", [128, TC])
        psden = B.ps(st, "l2_psden", [1, TC])
        psz = B.ps(st, "l2_psz", [128, TC])
        psw = psS
        B.barrier()
        for h in range(8):
            B.barrier()
            for tc in range(NTC):
                bank = tc % 2
                PE.wait(B.sACT.tok())
                B.sPE.inc(PE.matmul(out=psw[bank][:],
                                    lhsT=WUQ[0:33, h * 128:(h + 1) * 128],
                                    rhs=CQ[0:33, tc * TC:(tc + 1) * TC],
                                    start=True, stop=True))
                ACT.wait(B.sPE.tok())
                (ACT.copy(out=QC[:, tc * TC:(tc + 1) * TC], in_=psw[bank][:]))
                PE.wait(B.sACT.tok())
                B.sPE.inc(PE.matmul(out=psw[bank][0:32, :],
                                    lhsT=WQR[0:33, h * 32:(h + 1) * 32],
                                    rhs=CQ[0:33, tc * TC:(tc + 1) * TC],
                                    start=True, stop=True))
                ACT.wait(B.sPE.tok())
                (ACT.copy(out=QR[0:32, tc * TC:(tc + 1) * TC], in_=psw[bank][0:32, :]))
            for kc8 in range(2 * T // TC):
                bank = kc8 % 2
                PE.wait(B.sACT.tok())
                B.sPE.inc(PE.matmul(out=psw[bank][:],
                                    lhsT=WUK[0:33, h * 128:(h + 1) * 128],
                                    rhs=CKVF[0:33, kc8 * TC:(kc8 + 1) * TC],
                                    start=True, stop=True))
                ACT.wait(B.sPE.tok())
                (ACT.copy(out=KC[:, kc8 * TC:(kc8 + 1) * TC], in_=psw[bank][:]))
            for kt in range(NKT):
                bank = kt % 2
                PE.wait(B.sACT.tok())
                B.sPE.inc(PE.matmul(out=psw[bank][:, 0:128],
                                    lhsT=CKVF[0:33, kt * 128:(kt + 1) * 128],
                                    rhs=WUV[0:33, h * 128:(h + 1) * 128],
                                    start=True, stop=True))
                ACT.wait(B.sPE.tok())
                (ACT.copy(out=VTM[:, kt * 128:(kt + 1) * 128], in_=psw[bank][:, 0:128]))
            B.barrier()
            for qc in range(NTC):
                qcol = qc * TC
                freeS = getattr(B, "_l2fS", [None] * 4)
                freeP = getattr(B, "_l2fP", [None] * 4)
                B._l2fS, B._l2fP = freeS, freeP
                petoks = {}

                def emit_mm2(j):
                    bj = j % 4
                    PE.wait(freeS[bj])
                    PE.matmul(out=psS[bj][:], lhsT=KC[:, j * 128:(j + 1) * 128],
                              rhs=QC[:, qcol:qcol + TC], start=True, stop=False)
                    B.sPE.inc(PE.matmul(out=psS[bj][:],
                                        lhsT=KRF[0:32, j * 128:(j + 1) * 128],
                                        rhs=QR[0:32, qcol:qcol + TC],
                                        start=False, stop=True))
                    petoks[j] = B.sPE.tok()

                emit_mm2(0)
                emit_mm2(1)
                emit_mm2(2)
                for kt in range(NKT):
                    bank = kt % 4
                    ACT.wait(petoks[kt])
                    ACT.wait(freeP[bank])
                    (ACT.activation(out=PM[bank][:], in_=psS[bank][:], func=AF.Exp,
                                    scale=L2_SCALE))
                    freeS[bank] = B.sACT.tok()
                    if kt + 3 < NKT:
                        emit_mm2(kt + 3)
                    PE.wait(B.sACT.tok())
                    B.sPE.inc(PE.matmul(out=psO[:], lhsT=VTM[:, kt * 128:(kt + 1) * 128],
                                        rhs=PM[bank][:], start=(kt == 0),
                                        stop=(kt == NKT - 1)))
                    B.sPE.inc(PE.matmul(out=psden[:], lhsT=B.c_ones16[:, 0:1],
                                        rhs=PM[bank][:], start=(kt == 0),
                                        stop=(kt == NKT - 1)))
                    freeP[bank] = B.sPE.tok()
                DVE.wait(B.sPE.tok())
                den = rows[0:1, 0:TC]
                (DVE.reciprocal(out=den, in_=psden[0:1, :]))
                B.split_hl(den, rhl[0:1, 0:TC], rhl[0:1, TC:2 * TC])
                PE.wait(B.sDVE.tok())
                B.bcast_hl(psz[:], rhl[0:1, 0:TC], rhl[0:1, TC:2 * TC])
                ACT.wait(B.sPE.tok())
                (ACT.copy(out=ZB[:], in_=psz[:]))
                DVE.wait(B.sACT.tok())
                (DVE.tensor_tensor(out=OT[:, h * T + qcol:h * T + qcol + TC],
                                   in0=psO[:], in1=ZB[:], op=ALU.mult))
                PE.wait(B.sDVE.tok())
    bo = B.sb(st_att, "l2_bo_sb", [128, 8], side="right")
    B.barrier()
    B.dma(SP, B.sDX, bo[:], B._dram["l2_bo"][:, :])
    wo = B._dram["l2_wo"]
    B.barrier()

    def ev_s(oc, tc, bank, pe_tok):
        ACT.wait(pe_tok)
        col = oc * T + tc * TC
        (ACT.activation(out=B.S[:, col:col + TC], in_=bank[:], func=AF.Identity,
                        bias=bo[:, oc:oc + 1]))
        return B.sACT.tok()

    with B.scope() as sto:
        B.fm_proj(sto, "l2o", lambda oc: wo[:, oc * 1024:(oc + 1) * 1024],
                  NDC, NDC,
                  lambda kc, tc: OT[:, kc * T + tc * TC:kc * T + tc * TC + TC],
                  ev_s)
    B.barrier()
    st_att.close()


def emit_l2_ffn(B, lst):
    PE, DVE, ACT, POOL = B.PE, B.DVE, B.ACT, B.POOL
    NHC = 32
    TG = 512
    NTG = T // TG
    LGH = B.sb(lst, "l2_lgh", [128, 8 * (T // 128)], BF16, side="right")
    LGL = B.sb(lst, "l2_lgl", [128, 8 * (T // 128)], BF16, side="right")
    gebrow = B.sb(lst, "l2_gebrow", [1, 8], side="right")
    gebhl = B.sb(lst, "l2_gebhl", [1, 16], BF16, side="right")
    B.dma(B.SP, B.sDX, gebrow[:], B._dram["l2_geb"][:, :])
    B.barrier()
    B.split_hl(gebrow[0:1, :], gebhl[0:1, 0:8], gebhl[0:1, 8:16])
    B.barrier()
    with B.scope() as st:
        emit_router(B, st, B._dram["l2_gw"], 8, 2,
                    lambda dc, tt: B.XN[:, dc * T + tt * 128:dc * T + tt * 128 + 128],
                    (gebhl[0:1, 0:8], gebhl[0:1, 8:16]), LGH, LGL,
                    renorm=True, renorm_eps=1e-6, name="l2rt")
    wud, wdd = B._dram["l2_wup"], B._dram["l2_wdn"]
    for tg in range(NTG):
        with B.scope() as stg:
            hbuf = B.sb(stg, "l2_h", [128, NHC * TG], BF16)
            gacc = B.sb(stg, "l2_gacc", [128, NDC * TG])
            WB = B.sb(stg, "l2_wbt", [128, TG])
            ytmp = B.sb(stg, "l2_ytmp", [128, TG])
            wrow_sc = B.sb(stg, "l2_wrsc", [1, 2 * TG], BF16, side="right")
            pswb = B.ps(stg, "l2_pswb", [128, TG])
            pswt = B.ps(stg, "l2_pswt", [1, 256], BF16)
            for e in range(8):
                B.barrier()
                build_wb(B, e, tg, TG, 8, LGH, LGL, wrow_sc, pswt, pswb, WB)
                with B.scope() as ste:
                    def ev_h(hc, tcz, bank, pe_tok):
                        ACT.wait(pe_tok)
                        (ACT.activation(out=hbuf[:, hc * TG:(hc + 1) * TG],
                                        in_=bank[:], func=AF.Gelu))
                        return B.sACT.tok()

                    B.fm_proj(ste, "l2up",
                              lambda hc, e=e: wud[e, :, hc * 1024:(hc + 1) * 1024],
                              NDC, NHC,
                              lambda kc, tcz, tg=tg: xn_chunk(B, kc, tg),
                              ev_h, ntc=1)
                with B.scope() as ste2:
                    def ev_y(oc, tcz, bank, pe_tok, e=e):
                        DVE.wait(pe_tok)
                        col = oc * TG
                        if e == 0:
                            (DVE.scalar_tensor_tensor(
                                out=gacc[:, col:col + TG], in0=bank[:], scalar=0.0,
                                in1=WB[:], op0=ALU.add, op1=ALU.mult))
                        else:
                            (DVE.scalar_tensor_tensor(
                                out=ytmp[:], in0=bank[:], scalar=0.0,
                                in1=WB[:], op0=ALU.add, op1=ALU.mult))
                            (DVE.tensor_tensor(out=gacc[:, col:col + TG],
                                               in0=gacc[:, col:col + TG], in1=ytmp[:],
                                               op=ALU.add))
                        return B.sDVE.tok()

                    B.fm_proj(ste2, "l2dn",
                              lambda oc, e=e: wdd[e, :, oc * 4096:(oc + 1) * 4096],
                              NHC, NDC,
                              lambda hc, tcz: hbuf[:, hc * TG:(hc + 1) * TG],
                              ev_y, ntc=1)
            B.barrier()
            with B.scope() as stn:
                emit_ln_stats(B, stn,
                              lambda dc, tcz: gacc[:, dc * TG:(dc + 1) * TG],
                              False, 1e-5, xn_out=gacc,
                              x32=lambda dc, tcz: gacc[:, dc * TG:(dc + 1) * TG],
                              name="l2gn", tcs=[0], xn_stride=TG)
            B.barrier()
            with B.scope() as stf:
                gt = B.sb(stf, "l2_gt", [128, TG])
                for dc in range(NDC):
                    col = dc * T + tg * TG
                    (DVE.tensor_tensor(out=gt[:], in0=gacc[:, dc * TG:(dc + 1) * TG],
                                       in1=B.XN[:, col:col + TG], op=ALU.add))
                    ACT.wait(B.sDVE.tok())
                    (ACT.activation(out=gt[:], in_=gt[:], func=AF.Silu))
                    DVE.wait(B.sACT.tok())
                    (DVE.tensor_tensor(out=gt[:], in0=gt[:], in1=B.S[:, col:col + TG],
                                       op=ALU.mult))
                    (DVE.tensor_tensor(out=B.X[:, col:col + TG],
                                       in0=B.X[:, col:col + TG], in1=gt[:],
                                       op=ALU.add))
                B.barrier()
    B.barrier()


def emit_layer2(B):
    with B.scope() as lst:
        st_xn = B.scope()
        B.XN = B.sb(st_xn, "l2_xn", [128, NDC * T], BF16)

        def close_xn():
            st_xn.close()
            B.XN = None
        B.close_xn = close_xn
        with B.scope() as st:
            emit_ln_stats(B, st, lambda dc, tc: x_chunk(B, dc, tc), False,
                          LN_EPS_OUTER, xn_out=B.XN,
                          x32=lambda dc, tc: x_chunk(B, dc, tc), name="l2ln")
        B.barrier()
        emit_l2_attn(B, lst)
        st_xn2 = B.scope()
        B.XN = B.sb(st_xn2, "l2_xn2", [128, NDC * T], BF16)
        with B.scope() as st:
            emit_ln_stats(B, st, lambda dc, tc: x_chunk(B, dc, tc), False,
                          LN_EPS_OUTER, xn_out=B.XN,
                          x32=lambda dc, tc: x_chunk(B, dc, tc), name="l2ln2")
        B.barrier()
        emit_l2_ffn(B, lst)
        st_xn2.close()
        B.XN = None
    B.barrier()





def host_prep_l3(B, inp):
    cols = lambda v: np.ascontiguousarray(
        np.asarray(v).reshape(-1).reshape(NDC, 128).T.astype(np.float32))
    B.din("l3_cols", np.concatenate(
        [cols(inp["L3_attn_ln1_g"]), cols(inp["L3_attn_ln1_b"]),
         cols(inp["L3_attn_ln2_g"]), cols(inp["L3_attn_ln2_b"]),
         cols(inp["L3_attn_time_decay"]), cols(inp["L3_attn_time_first"]),
         cols(inp["L3_attn_tm_k"]), cols(inp["L3_attn_tm_v"]),
         cols(inp["L3_attn_tm_r"]), cols(inp["L3_attn_cm_k"]),
         cols(inp["L3_attn_cm_r"])], axis=1))
    B.din("l3_flag", [np.full((128, 2), float(c % 2), np.float32)
                      for c in range(NCORE)])
    B.din("l3_wk", pack_km(tp(inp["L3_attn_Wk"])))
    B.din("l3_wv", pack_km(tp(inp["L3_attn_Wv"])))
    B.din("l3_wr", pack_km(tp(inp["L3_attn_Wr"])))
    B.din("l3_wo", pack_km(tp(inp["L3_attn_Wo"])))
    B.din("l3_wck", pack_km(tp(inp["L3_attn_Wck"])))
    B.din("l3_wcv", pack_km(tp(inp["L3_attn_Wcv"])))
    B.din("l3_wcr", pack_km(tp(inp["L3_attn_Wcr"])))
    B.din("l3_rw", pack_km(tp(inp["L3_ffn_router_w"]), mtile=32))
    B.din("l3_rb", np.asarray(inp["L3_ffn_router_bias"]).reshape(1, 32).astype(np.float32))
    gw = np.asarray(inp["L3_ffn_gate_w"]); uw = np.asarray(inp["L3_ffn_up_w"])
    dw = np.asarray(inp["L3_ffn_down_w"])
    B.din("l3_gw", np.stack([pack_km(tp(gw[e])) for e in range(32)]))
    B.din("l3_uw", np.stack([pack_km(tp(uw[e])) for e in range(32)]))
    B.din("l3_dw", np.stack([pack_km(tp(dw[e])) for e in range(32)]))
    B.din("l3_sg", pack_km(tp(inp["L3_ffn_sgate_w"])))
    B.din("l3_su", pack_km(tp(inp["L3_ffn_sup_w"])))
    B.din("l3_sd", pack_km(tp(inp["L3_ffn_sdown_w"])))


def l3_mix(B, st, H, MIX, tmc, bnd, name):
    DVE = B.DVE
    d = B.sb(st, name + "_d", [128, TC])
    for dc in range(NDC):
        for tc in range(NTC):
            col = dc * T + tc * TC
            hs = H[:, col - 1:col - 1 + TC] if col > 0 else H[:, col:col + TC]
            DVE.wait(B.sPE.tok())
            (DVE.tensor_tensor(out=d[:], in0=H[:, col:col + TC], in1=hs,
                               op=ALU.subtract))
            (DVE.scalar_tensor_tensor(out=MIX[:, col:col + TC], in0=d[:],
                                      scalar=tmc(dc), in1=hs, op0=ALU.mult,
                                      op1=ALU.add))
        c0 = dc * T
        (DVE.tensor_tensor(out=d[:, 0:1], in0=H[:, c0:c0 + 1],
                           in1=bnd[:, dc:dc + 1], op=ALU.subtract))
        (DVE.scalar_tensor_tensor(out=MIX[:, c0:c0 + 1], in0=d[:, 0:1],
                                  scalar=tmc(dc), in1=bnd[:, dc:dc + 1],
                                  op0=ALU.mult, op1=ALU.add))
    return B.sDVE.tok()


def l3_boundary(B, st, H, BND, flagcol, ag_in, ag_out, name):
    DVE, ACT, POOL, SP = B.DVE, B.ACT, B.POOL, B.SP
    scc = B.cc_sem()
    lastc = B.sb(st, name + "_lc", [128, NDC])
    ACT.wait(B.sDVE.tok())
    for dc in range(NDC):
        (ACT.copy(out=lastc[:, dc:dc + 1], in_=H[:, dc * T + T - 1:dc * T + T]))
    POOL.wait(B.sACT.tok())
    B.dma(POOL, B.sDW, ag_in[:, :], lastc[:])
    POOL.wait(B.sDW.tok())
    scc.inc(POOL.collective_compute(
        "AllGather", ALU.bypass, replica_groups=PAIRS,
        ins=[ag_in[:, :]], outs=[ag_out[:, :]]))
    POOL.wait(scc.tok())
    SP.wait(scc.tok())
    B.dma(SP, B.sDX, lastc[:], ag_out[0:128, :])
    DVE.wait(B.sDX.tok())
    (DVE.tensor_scalar(out=BND[:], in0=lastc[:], scalar1=flagcol, scalar2=None,
                       op0=ALU.mult))
    return B.sDVE.tok()


def l3_mixproj(B, lst_scope, H, BND1, cols, mixcol, wname, name, evac,
               spill_dram=None, spill_act=AF.Copy):
    ACT, POOL = B.ACT, B.POOL
    with B.scope() as st:
        XM = B.sb(st, "l3_xm" + name, [128, NDC * T], BF16)
        KO = B.sb(st, "l3_ko" + name, [128, T], BF16) if spill_dram is not None else None
        l3_mix(B, st, H, XM, lambda dc: cols[:, mixcol + dc:mixcol + dc + 1],
               BND1, "l3m" + name)
        B.barrier()

        if spill_dram is not None:
            def ev(oc, tc, bank, pe_tok):
                ACT.wait(pe_tok)
                if tc == 0:
                    ACT.wait(B.sSPL.tok())
                (ACT.activation(out=KO[:, tc * TC:(tc + 1) * TC], in_=bank[:],
                                func=spill_act))
                if tc == NTC - 1:
                    POOL.wait(B.sACT.tok())
                    B.dma(POOL, B.sSPL, spill_dram[:, oc * T:(oc + 1) * T], KO[:])
                return B.sACT.tok()
        else:
            ev = evac

        B.fm_proj(st, "l3p" + name,
                  lambda oc: B._dram[wname][:, oc * 1024:(oc + 1) * 1024],
                  NDC, NDC,
                  lambda kc, tc: XM[:, kc * T + tc * TC:kc * T + tc * TC + TC],
                  ev)
    B.barrier()


def emit_l3_attn(B, lst):
    PE, DVE, ACT, POOL, SP = B.PE, B.DVE, B.ACT, B.POOL, B.SP
    cols = B.sb(lst, "l3_cols_sb", [128, 88], side="right")
    flag = B.sb(lst, "l3_flag_sb", [128, 2], side="right")
    B.dma(SP, B.sDX, cols[:], B._dram["l3_cols"][:, :])
    B.dma(SP, B.sDX, flag[:], B._dram["l3_flag"][:, :])
    xn_d = B.dint("l3_xn_d", [128, NDC * T], BF16)
    k_d = B.dint("l3_k_d", [128, NDC * T], BF16)
    v_d = B.dint("l3_v_d", [128, NDC * T], BF16)
    rw_d = B.dint("l3_rw_d", [128, NDC * T], BF16)
    bx_in = B.dint("l3_bx_in", [128, NDC])
    bx_out = B.dint("l3_bx_out", [256, NDC])
    bx2_in = B.dint("l3_bx2_in", [128, NDC])
    bx2_out = B.dint("l3_bx2_out", [256, NDC])
    fin_in = B.dint("l3_fin_in", [128, 2 * NDC])
    fin_out = B.dint("l3_fin_out", [256, 2 * NDC])
    sccf = B.cc_sem()

    POOL.wait(B.sDVE.tok())
    B.dma(POOL, B.sDW, xn_d[:, :], B.XN[:])
    POOL.wait(B.sDW.tok())
    B.barrier()

    sr_d = B.dint("l3_sr_d", [128, NDC * T], BF16)
    st_h1 = B.scope()
    H1 = B.sb(st_h1, "l3_h1", [128, NDC * T], BF16, side="right")
    with B.scope() as st:
        emit_ln_stats(B, st, lambda dc, tc: xn_chunk(B, dc, tc), True, 1e-5,
                      xn_out=H1, x32=lambda dc, tc: xn_chunk(B, dc, tc),
                      name="l3h1",
                      gb=lambda dc: (cols[:, dc:dc + 1], cols[:, 8 + dc:8 + dc + 1]))
    B.barrier()
    B.close_xn()
    BND1 = B.sb(st_h1, "l3_bnd1", [128, NDC], side="right")
    with B.scope() as st:
        l3_boundary(B, st, H1, BND1, flag[:, 0:1], bx_in, bx_out, "l3b1")
    B.barrier()

    l3_mixproj(B, lst, H1, BND1, cols, 64, "l3_wr", "r", None,
               spill_dram=sr_d, spill_act=AF.Sigmoid)
    l3_mixproj(B, lst, H1, BND1, cols, 48, "l3_wk", "k", None, spill_dram=k_d)
    l3_mixproj(B, lst, H1, BND1, cols, 56, "l3_wv", "v", None, spill_dram=v_d)
    st_h1.close()

    with B.scope() as st:
        KG = B.sb(st, "l3_kg", [128, T], BF16)
        VG = B.sb(st, "l3_vg", [128, T], BF16)
        EK = B.sb(st, "l3_ek", [128, T])
        EKV = B.sb(st, "l3_ekv", [128, T])
        As = B.sb(st, "l3_as", [128, T + 2])
        Bs = B.sb(st, "l3_bs", [128, T + 2])
        EW = B.sb(st, "l3_ew", [128, T])
        ONESF = B.sb(st, "l3_onesf", [128, T])
        NUM = B.sb(st, "l3_num", [128, T])
        DEN = B.sb(st, "l3_den", [128, T])
        RWG = B.sb(st, "l3_rwg", [128, T], BF16)
        SRG = B.sb(st, "l3_srg", [128, T], BF16)
        FIN = B.sb(st, "l3_fin", [128, 2 * NDC])
        INI = B.sb(st, "l3_ini", [128, 2 * NDC])
        ECOL = B.sb(st, "l3_ecol", [128, 2 * NDC])
        B.barrier()
        (DVE.memset(ONESF[:], 1.0))
        ACT.wait(B.sDX.tok())
        (ACT.activation(out=ECOL[:, 0:NDC], in_=cols[:, 32:40], func=AF.Exp,
                        scale=1.0))
        DVE.wait(B.sACT.tok())
        (DVE.tensor_scalar(out=ECOL[:, 0:NDC], in0=ECOL[:, 0:NDC], scalar1=-1.0,
                           scalar2=None, op0=ALU.mult))
        ACT.wait(B.sDVE.tok())
        (ACT.activation(out=ECOL[:, 0:NDC], in_=ECOL[:, 0:NDC], func=AF.Exp))
        (ACT.activation(out=ECOL[:, NDC:2 * NDC], in_=cols[:, 40:48], func=AF.Exp))
        B.barrier()

        def scan_pass(g, ini_a, ini_b, emit_y):
            DVE.wait(B.sDX.tok())
            ACT.wait(B.sDX.tok())
            (ACT.activation(out=EK[:], in_=KG[:], func=AF.Exp))
            DVE.wait(B.sACT.tok())
            (DVE.tensor_tensor(out=EKV[:], in0=EK[:], in1=VG[:], op=ALU.mult))
            (DVE.tensor_scalar(out=EW[:], in0=ONESF[:],
                               scalar1=ECOL[:, g:g + 1], scalar2=None,
                               op0=ALU.mult))
            (DVE.tensor_copy(out=As[:, 0:1], in_=ini_a))
            (DVE.tensor_copy(out=Bs[:, 0:1], in_=ini_b))
            (DVE.tensor_tensor_scan(out=As[:, 1:T + 1], data0=EW[:], data1=EKV[:],
                                    initial=ini_a, op0=ALU.mult, op1=ALU.add))
            (DVE.tensor_tensor_scan(out=Bs[:, 1:T + 1], data0=EW[:], data1=EK[:],
                                    initial=ini_b, op0=ALU.mult, op1=ALU.add))
            if emit_y:
                (DVE.scalar_tensor_tensor(out=NUM[:], in0=EKV[:],
                                          scalar=ECOL[:, NDC + g:NDC + g + 1],
                                          in1=As[:, 0:T], op0=ALU.mult, op1=ALU.add))
                (DVE.scalar_tensor_tensor(out=DEN[:], in0=EK[:],
                                          scalar=ECOL[:, NDC + g:NDC + g + 1],
                                          in1=Bs[:, 0:T], op0=ALU.mult, op1=ALU.add))
                (DVE.reciprocal(out=DEN[:], in_=DEN[:]))
                (DVE.tensor_tensor(out=NUM[:], in0=NUM[:], in1=DEN[:], op=ALU.mult))
                (DVE.tensor_tensor(out=RWG[:], in0=NUM[:],
                                   in1=SRG[:], op=ALU.mult))
            return B.sDVE.tok()

        for g in range(NDC):
            SP.wait(B.sDVE.tok())
            B.dma(SP, B.sDX, KG[:], k_d[:, g * T:(g + 1) * T])
            B.dma(SP, B.sDX, VG[:], v_d[:, g * T:(g + 1) * T])
            scan_pass(g, B.c_zcol[:, 0:1], B.c_zcol[:, 0:1], False)
            (DVE.tensor_copy(out=FIN[:, g:g + 1], in_=As[:, T:T + 1]))
            (DVE.tensor_copy(out=FIN[:, NDC + g:NDC + g + 1], in_=Bs[:, T:T + 1]))
        POOL.wait(B.sDVE.tok())
        B.dma(POOL, B.sDW, fin_in[:, :], FIN[:])
        POOL.wait(B.sDW.tok())
        sccf.inc(POOL.collective_compute(
            "AllGather", ALU.bypass, replica_groups=PAIRS,
            ins=[fin_in[:, :]], outs=[fin_out[:, :]]))
        POOL.wait(sccf.tok())
        SP.wait(sccf.tok())
        B.dma(SP, B.sDX, INI[:], fin_out[0:128, :])
        DVE.wait(B.sDX.tok())
        (DVE.tensor_scalar(out=INI[:], in0=INI[:], scalar1=flag[:, 0:1],
                           scalar2=None, op0=ALU.mult))
        B.barrier()
        for g in range(NDC):
            SP.wait(B.sDVE.tok())
            SP.wait(B.sSPL.tok())
            B.dma(SP, B.sDX, KG[:], k_d[:, g * T:(g + 1) * T])
            B.dma(SP, B.sDX, VG[:], v_d[:, g * T:(g + 1) * T])
            B.dma(SP, B.sDX, SRG[:], sr_d[:, g * T:(g + 1) * T])
            scan_pass(g, INI[:, g:g + 1], INI[:, NDC + g:NDC + g + 1], True)
            POOL.wait(B.sDVE.tok())
            B.dma(POOL, B.sSPL, rw_d[:, g * T:(g + 1) * T], RWG[:])
        POOL.wait(B.sSPL.tok())
    B.barrier()

    with B.scope() as st:
        RW = B.sb(st, "l3_rwf", [128, NDC * T], BF16)
        XNR = B.sb(st, "l3_xnr", [128, NDC * T], BF16)
        B.barrier()
        B.dma(SP, B.sDX, RW[:], rw_d[:, :])
        B.dma(SP, B.sDX, XNR[:], xn_d[:, :])

        def ev_a1(oc, tc, bank, pe_tok):
            DVE.wait(pe_tok)
            DVE.wait(B.sDX.tok())
            col = oc * T + tc * TC
            (DVE.tensor_tensor(out=B.S[:, col:col + TC], in0=bank[:],
                               in1=XNR[:, col:col + TC], op=ALU.add))
            return B.sDVE.tok()

        B.fm_proj(st, "l3o",
                  lambda oc: B._dram["l3_wo"][:, oc * 1024:(oc + 1) * 1024],
                  NDC, NDC,
                  lambda kc, tc: RW[:, kc * T + tc * TC:kc * T + tc * TC + TC],
                  ev_a1)
    B.barrier()

    st_cr = B.scope()
    CR = B.sb(st_cr, "l3_cr", [128, NDC * T], BF16, side="right")
    st_h2 = B.scope()
    H2 = B.sb(st_h2, "l3_h2", [128, NDC * T], BF16, side="right")
    with B.scope() as st:
        emit_ln_stats(B, st, lambda dc, tc: B.S[:, dc * T + tc * TC:dc * T + tc * TC + TC],
                      True, 1e-5, xn_out=H2,
                      x32=lambda dc, tc: B.S[:, dc * T + tc * TC:dc * T + tc * TC + TC],
                      name="l3h2",
                      gb=lambda dc: (cols[:, 16 + dc:16 + dc + 1],
                                     cols[:, 24 + dc:24 + dc + 1]))
    B.barrier()
    BND2 = B.sb(st_h2, "l3_bnd2", [128, NDC], side="right")
    with B.scope() as st:
        l3_boundary(B, st, H2, BND2, flag[:, 0:1], bx2_in, bx2_out, "l3b2")
    B.barrier()

    def ev_cr(oc, tc, bank, pe_tok):
        ACT.wait(pe_tok)
        col = oc * T + tc * TC
        (ACT.activation(out=CR[:, col:col + TC], in_=bank[:], func=AF.Sigmoid))
        return B.sACT.tok()

    l3_mixproj(B, lst, H2, BND2, cols, 80, "l3_wcr", "cr", ev_cr)
    C2 = 256
    with B.scope() as st:
        XK2 = B.sb(st, "l3_xk2", [128, NDC * C2], BF16)
        KK = B.sb(st, "l3_kk", [128, 32 * C2], BF16)
        tmx = B.sb(st, "l3_tmx", [128, C2])
        d2 = B.sb(st, "l3_d2", [128, C2])
        for tc2 in range(T // C2):
            DVE.wait(B.sPE.tok())
            for dc in range(NDC):
                col = dc * T + tc2 * C2
                hs = H2[:, col - 1:col - 1 + C2] if col > 0 else H2[:, col:col + C2]
                (DVE.tensor_tensor(out=d2[:], in0=H2[:, col:col + C2], in1=hs,
                                   op=ALU.subtract))
                (DVE.scalar_tensor_tensor(out=XK2[:, dc * C2:(dc + 1) * C2],
                                          in0=d2[:],
                                          scalar=cols[:, 72 + dc:72 + dc + 1],
                                          in1=hs, op0=ALU.mult, op1=ALU.add))
                if tc2 == 0:
                    (DVE.tensor_tensor(out=d2[:, 0:1], in0=H2[:, col:col + 1],
                                       in1=BND2[:, dc:dc + 1], op=ALU.subtract))
                    (DVE.scalar_tensor_tensor(out=XK2[:, dc * C2:dc * C2 + 1],
                                              in0=d2[:, 0:1],
                                              scalar=cols[:, 72 + dc:72 + dc + 1],
                                              in1=BND2[:, dc:dc + 1],
                                              op0=ALU.mult, op1=ALU.add))
            with B.scope() as stk:
                def ev_kk(hc, tcz, bank, pe_tok):
                    DVE.wait(pe_tok)
                    DVE.wait(B.sACT.tok())
                    (DVE.tensor_scalar(out=tmx[:], in0=bank[0:128, 0:C2],
                                       scalar1=0.0, scalar2=None, op0=ALU.max))
                    ACT.wait(B.sDVE.tok())
                    (ACT.activation(out=KK[:, hc * C2:(hc + 1) * C2], in_=tmx[:],
                                    func=AF.Square))
                    return B.sACT.tok()

                B.fm_proj(stk, "l3ck",
                          lambda hc: B._dram["l3_wck"][:, hc * 1024:(hc + 1) * 1024],
                          NDC, 32,
                          lambda kc, tcz: XK2[:, kc * C2:(kc + 1) * C2],
                          ev_kk, ntc=1, ncol=C2)
            with B.scope() as stk2:
                def ev_kv(oc, tcz, bank, pe_tok, tc2=tc2):
                    DVE.wait(pe_tok)
                    col = oc * T + tc2 * C2
                    (DVE.tensor_tensor(out=tmx[:], in0=bank[0:128, 0:C2],
                                       in1=CR[:, col:col + C2], op=ALU.mult))
                    (DVE.tensor_tensor(out=B.S[:, col:col + C2],
                                       in0=B.S[:, col:col + C2], in1=tmx[:],
                                       op=ALU.add))
                    return B.sDVE.tok()

                B.fm_proj(stk2, "l3cv",
                          lambda oc: B._dram["l3_wcv"][:, oc * 4096:(oc + 1) * 4096],
                          32, NDC,
                          lambda hc, tcz: KK[:, hc * C2:(hc + 1) * C2],
                          ev_kv, ntc=1, ncol=C2)
    B.barrier()
    st_h2.close()
    st_cr.close()


def emit_l3_ffn(B, lst):
    PE, DVE, ACT, POOL = B.PE, B.DVE, B.ACT, B.POOL
    nE = 32
    TG = 512
    NTG = T // TG
    LGH = B.sb(lst, "l3_lgh", [128, nE * (T // 128)], BF16, side="right")
    LGL = B.sb(lst, "l3_lgl", [128, nE * (T // 128)], BF16, side="right")
    rbrow = B.sb(lst, "l3_rbrow", [1, 32], side="right")
    rbhl = B.sb(lst, "l3_rbhl", [1, 64], BF16, side="right")
    B.dma(B.SP, B.sDX, rbrow[:], B._dram["l3_rb"][:, :])
    B.barrier()
    B.split_hl(rbrow[0:1, :], rbhl[0:1, 0:32], rbhl[0:1, 32:64])
    B.barrier()
    with B.scope() as st:
        emit_router(B, st, B._dram["l3_rw"], nE, 4,
                    lambda dc, tt: B.XN[:, dc * T + tt * 128:dc * T + tt * 128 + 128],
                    (rbhl[0:1, 0:32], rbhl[0:1, 32:64]), LGH, LGL,
                    renorm=True, name="l3rt")
    gwd, uwd, dwd = B._dram["l3_gw"], B._dram["l3_uw"], B._dram["l3_dw"]
    for tg in range(NTG):
        with B.scope() as stg:
            gacc = B.sb(stg, "l3_gacc", [128, NDC * TG])
            WB = B.sb(stg, "l3_wbt", [128, TG])
            HE = B.sb(stg, "l3_he", [128, TG], BF16)
            ht = B.sb(stg, "l3_ht", [128, TG])
            ytmp = B.sb(stg, "l3_ytmp", [128, TG])
            wge = [B.sb(stg, f"l3_wge{i}", [128, 1024], BF16) for i in range(2)]
            wue = [B.sb(stg, f"l3_wue{i}", [128, 1024], BF16) for i in range(2)]
            wde = [B.sb(stg, f"l3_wde{i}", [128, 1024], BF16) for i in range(2)]
            wrow_sc = B.sb(stg, "l3_wrsc", [1, 2 * TG], BF16, side="right")
            pswb = B.ps(stg, "l3_pswb", [128, TG])
            pswt = B.ps(stg, "l3_pswt", [1, 256], BF16)
            psg = B.ps(stg, "l3_psg", [128, TG])
            psu = B.ps(stg, "l3_psu", [128, TG])
            psy = [B.ps(stg, f"l3_psy{i}", [128, TG]) for i in range(2)]
            wsem = [B.sDW, B.sDW2]
            B.barrier()
            wtok = {}
            for pre in range(2):
                B.dma(POOL, wsem[pre], wge[pre][:], gwd[pre, :, :])
                B.dma(POOL, wsem[pre], wue[pre][:], uwd[pre, :, :])
                wtok[pre] = B.dma(POOL, wsem[pre], wde[pre][:], dwd[pre, :, :])
            for e in range(nE):
                slot = e % 2
                build_wb(B, e, tg, TG, nE, LGH, LGL, wrow_sc, pswt, pswb, WB)
                PE.wait(wtok[e])
                PE.wait(B.sACT.tok())
                PE.wait(B.sDVE.tok())
                tiles = [(wge[slot][:, kc * 128:(kc + 1) * 128],
                          xn_chunk(B, kc, tg)) for kc in range(NDC)]
                B.mm_acc(psg[:], tiles, sem=B.sPE)
                tiles = [(wue[slot][:, kc * 128:(kc + 1) * 128],
                          xn_chunk(B, kc, tg)) for kc in range(NDC)]
                B.mm_acc(psu[:], tiles, sem=B.sPE)
                ACT.wait(B.sPE.tok())
                (ACT.activation(out=ht[:], in_=psg[:], func=AF.Silu))
                DVE.wait(B.sACT.tok())
                (DVE.tensor_tensor(out=HE[:], in0=ht[:], in1=psu[:], op=ALU.mult))
                for oc in range(NDC):
                    bank = oc % 2
                    PE.wait(B.sDVE.tok())
                    B.sPE.inc(PE.matmul(out=psy[bank][:],
                                        lhsT=wde[slot][:, oc * 128:(oc + 1) * 128],
                                        rhs=HE[:], start=True, stop=True))
                    DVE.wait(B.sPE.tok())
                    col = oc * TG
                    if e == 0:
                        (DVE.scalar_tensor_tensor(
                            out=gacc[:, col:col + TG], in0=psy[bank][:], scalar=0.0,
                            in1=WB[:], op0=ALU.add, op1=ALU.mult))
                    else:
                        (DVE.scalar_tensor_tensor(
                            out=ytmp[:], in0=psy[bank][:], scalar=0.0,
                            in1=WB[:], op0=ALU.add, op1=ALU.mult))
                        (DVE.tensor_tensor(out=gacc[:, col:col + TG],
                                           in0=gacc[:, col:col + TG], in1=ytmp[:],
                                           op=ALU.add))
                if e + 2 < nE:
                    POOL.wait(B.sPE.tok())
                    B.dma(POOL, wsem[slot], wge[slot][:], gwd[e + 2, :, :])
                    B.dma(POOL, wsem[slot], wue[slot][:], uwd[e + 2, :, :])
                    wtok[e + 2] = B.dma(POOL, wsem[slot], wde[slot][:], dwd[e + 2, :, :])
            B.barrier()
            with B.scope() as sts:
                HS = B.sb(sts, "l3_hs", [128, 4 * TG], BF16)
                wsg = [B.sb(sts, f"l3_wsg{i}", [128, 1024], BF16) for i in range(2)]
                wsu = [B.sb(sts, f"l3_wsu{i}", [128, 1024], BF16) for i in range(2)]
                sgd, sud, sdd = B._dram["l3_sg"], B._dram["l3_su"], B._dram["l3_sd"]
                B.barrier()
                stok = {}
                for pre in range(2):
                    B.dma(POOL, wsem[pre], wsg[pre][:],
                          sgd[:, pre * 1024:(pre + 1) * 1024])
                    stok[pre] = B.dma(POOL, wsem[pre], wsu[pre][:],
                                      sud[:, pre * 1024:(pre + 1) * 1024])
                for hc in range(4):
                    slot = hc % 2
                    PE.wait(stok[hc])
                    PE.wait(B.sACT.tok())
                    PE.wait(B.sDVE.tok())
                    tiles = [(wsg[slot][:, kc * 128:(kc + 1) * 128],
                              xn_chunk(B, kc, tg)) for kc in range(NDC)]
                    B.mm_acc(psg[:], tiles, sem=B.sPE)
                    tiles = [(wsu[slot][:, kc * 128:(kc + 1) * 128],
                              xn_chunk(B, kc, tg)) for kc in range(NDC)]
                    B.mm_acc(psu[:], tiles, sem=B.sPE)
                    ACT.wait(B.sPE.tok())
                    (ACT.activation(out=ht[:], in_=psg[:], func=AF.Silu))
                    DVE.wait(B.sACT.tok())
                    (DVE.tensor_tensor(out=HS[:, hc * TG:(hc + 1) * TG], in0=ht[:],
                                       in1=psu[:], op=ALU.mult))
                    if hc + 2 < 4:
                        POOL.wait(B.sPE.tok())
                        B.dma(POOL, wsem[slot], wsg[slot][:],
                              sgd[:, (hc + 2) * 1024:(hc + 3) * 1024])
                        stok[hc + 2] = B.dma(POOL, wsem[slot], wsu[slot][:],
                                             sud[:, (hc + 2) * 1024:(hc + 3) * 1024])

                def ev_sd(oc, tcz, bank, pe_tok):
                    DVE.wait(pe_tok)
                    col = oc * TG
                    (DVE.tensor_tensor(out=gacc[:, col:col + TG],
                                       in0=gacc[:, col:col + TG], in1=bank[:],
                                       op=ALU.add))
                    return B.sDVE.tok()

                B.fm_proj(sts, "l3sd",
                          lambda oc: sdd[:, oc * 512:(oc + 1) * 512],
                          4, NDC,
                          lambda hc, tcz: HS[:, hc * TG:(hc + 1) * TG],
                          ev_sd, ntc=1)
            B.barrier()
            with B.scope() as stf:
                gt = B.sb(stf, "l3_gt", [128, TG])
                for dc in range(NDC):
                    (ACT.activation(out=gt[:], in_=gacc[:, dc * TG:(dc + 1) * TG],
                                    func=AF.Silu))
                    DVE.wait(B.sACT.tok())
                    col = dc * T + tg * TG
                    (DVE.tensor_tensor(out=gt[:], in0=gt[:], in1=B.S[:, col:col + TG],
                                       op=ALU.mult))
                    (DVE.tensor_tensor(out=B.X[:, col:col + TG],
                                       in0=B.X[:, col:col + TG], in1=gt[:],
                                       op=ALU.add))
                    ACT.wait(B.sDVE.tok())
                B.barrier()
    B.barrier()


def emit_layer3(B):
    with B.scope() as lst:
        st_xn = B.scope()
        B.XN = B.sb(st_xn, "l3_xn", [128, NDC * T], BF16)

        def close_xn():
            st_xn.close()
            B.XN = None
        B.close_xn = close_xn
        with B.scope() as st:
            emit_ln_stats(B, st, lambda dc, tc: x_chunk(B, dc, tc), False,
                          LN_EPS_OUTER, xn_out=B.XN,
                          x32=lambda dc, tc: x_chunk(B, dc, tc), name="l3ln")
        B.barrier()
        emit_l3_attn(B, lst)
        st_xn2 = B.scope()
        B.XN = B.sb(st_xn2, "l3_xn2", [128, NDC * T], BF16)
        B.barrier()
        B.dma(B.SP, B.sDX, B.XN[:], B._dram["l3_xn_d"][:, :])
        B.barrier()
        emit_l3_ffn(B, lst)
        st_xn2.close()
        B.XN = None
    B.barrier()


LAYER_EMITTERS = {0: emit_layer0, 1: emit_layer1, 2: emit_layer2, 3: emit_layer3}
LAYER_PREPS = {0: host_prep_l0, 1: host_prep_l1, 2: host_prep_l2, 3: host_prep_l3}



def build_program(inp, layers, x_feeds):
    B = Builder()
    bf = ml_dtypes.bfloat16
    B.din("c_ones16", np.ones((128, 1), bf), BF16)
    B.din("c_ones1f", np.ones((1, 128), np.float32))
    B.din("c_ones1b", np.ones((1, 128), bf), BF16)
    B.din("c_id16", np.eye(128, dtype=bf), BF16)
    B.din("c_id32", np.eye(128, dtype=np.float32))
    B.din("c_id64s", np.vstack([np.eye(64, dtype=bf)] * 2), BF16)
    B.din("c_cvals", np.tile(np.asarray(CONST_VALS, np.float32)[None, :], (128, 1)))
    B.din("c_zcol", np.zeros((128, 2), np.float32))
    B.din("xin", x_feeds)
    xout = B.dout("xout", [128, NDC * T])
    for li in layers:
        LAYER_PREPS[li](B, inp)

    with B.scope() as top:
        B.X = B.sb(top, "X", [128, NDC * T])
        B.S = B.sb(top, "S", [128, NDC * T], BF16)
        dram = B._dram
        cones16 = B.sb(top, "c_ones16_sb", [128, 1], BF16)
        cones1f = B.sb(top, "c_ones1f_sb", [1, 128])
        cones1b = B.sb(top, "c_ones1b_sb", [1, 128], BF16)
        cid16 = B.sb(top, "c_id16_sb", [128, 128], BF16)
        cid32 = B.sb(top, "c_id32_sb", [128, 128])
        cid64 = B.sb(top, "c_id64s_sb", [128, 64], BF16)
        ccv = B.sb(top, "c_cvals_sb", [128, len(CONST_VALS)])
        czc = B.sb(top, "c_zcol_sb", [128, 2])
        for i, v in enumerate(CONST_VALS):
            B.nc.const_aps.register(F32, v, ccv[:, i:i + 1])
        B.dma(B.SP, B.sDX, ccv[:], dram["c_cvals"][:, :])
        B.c_ones16, B.c_ones1f, B.c_id16, B.c_id32 = cones16, cones1f, cid16, cid32
        B.c_ones1b = cones1b
        B.dma(B.SP, B.sDX, cones1b[:], dram["c_ones1b"][:, :])
        B.c_id64s = cid64
        B.c_zcol = czc
        B.dma(B.SP, B.sDX, czc[:], dram["c_zcol"][:, :])
        B.dma(B.SP, B.sDX, cid64[:], dram["c_id64s"][:, :])
        B.dma(B.SP, B.sDX, B.X[:], dram["xin"][:, :])
        B.dma(B.SP, B.sDX, cones16[:], dram["c_ones16"][:, :])
        B.dma(B.SP, B.sDX, cones1f[:], dram["c_ones1f"][:, :])
        B.dma(B.SP, B.sDX, cid16[:], dram["c_id16"][:, :])
        B.dma(B.SP, B.sDX, cid32[:], dram["c_id32"][:, :])
        B.barrier()
        for li in layers:
            LAYER_EMITTERS[li](B)
        B.barrier()
        if DEV_DUMP == "S":
            B.dma(B.POOL, B.sOUT, xout[:, :], B.S[:])
            B.POOL.wait(B.sOUT.tok())
        else:
            B.dma(B.SP, B.sOUT, xout[:, :], B.X[:])
            B.SP.wait(B.sOUT.tok())
    B.flush()
    B.stack.close()
    return B


def run_layers(inp, layers_groups, x0=None):
    if x0 is None:
        x0 = np.asarray(inp["L0_attn_x"])
    xp = [pack_x_half(x0, c) for c in range(NCORE)]
    for layers in layers_groups:
        B = build_program(inp, layers, xp)
        in_maps = [{k: v[c] for k, v in B.feeds.items()} for c in range(NCORE)]
        res = run_bass_kernel_spmd(B.nc, in_maps, core_ids=list(range(NCORE)))
        xp = [np.asarray(res.results[c]["xout"]) for c in range(NCORE)]
    return xp


def kernel(**inp):
    xp = run_layers(inp, [[0, 1, 2, 3]])
    out = np.empty((4, 4096, 1024), np.float32)
    for c in range(NCORE):
        b, h = c // 2, c % 2
        out[b, h * T:(h + 1) * T, :] = unpack_x_half(xp[c])
    return out
